# version 04
# speedup vs baseline: 1.2576x; 1.2521x over previous
import contextlib
import numpy as np
import concourse.bass as bass
import concourse.mybir as mybir
from concourse.bass_utils import run_bass_kernel_spmd

F32 = mybir.dt.float32
BF16 = mybir.dt.bfloat16
I32 = mybir.dt.int32
AF = mybir.ActivationFunctionType
ALU = mybir.AluOpType
AX = mybir.AxisListType

NCORE = 8
T = 2048
D = 1024
DC = D // 128
HID = 4096
HC = HID // 128
EPS = 1e-5
ENGS = ("sync", "tensor", "vector", "scalar", "gpsimd")



def blk_w(w, K=128):
    O, I = w.shape
    assert O % 128 == 0 and I % K == 0, (O, I, K)
    OC, IC = O // 128, I // K
    return np.ascontiguousarray(w.reshape(OC, 128, IC, K).transpose(0, 3, 2, 1))


def colpack(v):
    return np.ascontiguousarray(v.reshape(-1, 128).T)


def fm(x):
    return np.ascontiguousarray(x.T)



class LSem:
    __slots__ = ("h", "base", "n")

    def __init__(self, h, base):
        self.h = h
        self.base = base
        self.n = 0

    def inc(self, instr, k=1):
        instr.then_inc(self.h, k)
        self.n += k
        return instr

    def wait(self, eng, rel):
        eng.wait_ge(self.h, self.base + rel)


class Cx:
    def __init__(self, name):
        self.name = name
        self.nc = bass.Bass("TRN2", target_bir_lowering=False)
        self.st = contextlib.ExitStack()
        self._n = 0
        self._free = []
        self._live = []

    def sem(self, tag="s", pinned=False, fresh=False):
        if self._free and not fresh:
            h, top = self._free.pop()
        else:
            self._n += 1
            h, top = self.st.enter_context(self.nc.semaphore(f"{tag}{self._n}")), 0
        ls = LSem(h, top)
        if not pinned:
            self._live.append(ls)
        return ls

    def sems(self, k, tag="s", pinned=False, fresh=False):
        return [self.sem(tag, pinned, fresh) for _ in range(k)]

    def sem_on(self, prev):
        ls = LSem(prev.h, prev.base + prev.n)
        self._live.append(ls)
        return ls

    def checkpoint(self):
        tops = {}
        for ls in self._live:
            v = ls.base + ls.n
            if id(ls.h) not in tops or v > tops[id(ls.h)][1]:
                tops[id(ls.h)] = (ls.h, v)
        for h, v in tops.values():
            self._free.append((h, v))
        self._live = []

    def sb(self, tag, shape, dt):
        self._n += 1
        return self.st.enter_context(
            self.nc.sbuf_tensor(f"sb_{tag}{self._n}", list(shape), dt))

    def ps(self, tag="pb", shape=(128, 512)):
        self._n += 1
        return self.st.enter_context(
            self.nc.psum_tensor(f"ps_{tag}{self._n}", list(shape), F32))

    def din(self, name, shape, dt=F32):
        return self.nc.dram_tensor(name, list(shape), dt, kind="ExternalInput").ap()

    def dout(self, name, shape, dt=F32):
        return self.nc.dram_tensor(name, list(shape), dt, kind="ExternalOutput").ap()

    def dtmp(self, name, shape, dt):
        self._n += 1
        return self.nc.dram_tensor(f"dr_{name}{self._n}", list(shape), dt,
                                   kind="Internal").ap()

    def block(self):
        return self.st.enter_context(self.nc.Block())


def barrier(cx, block, tag="b"):
    s = cx.sem(f"bar_{tag}")

    def mk(eng):
        s.inc(eng.drain(), 1)
        s.wait(eng, len(ENGS))

    for name in ENGS:
        getattr(block, name)(mk)


def tslices(total, w=512):
    out, off = [], 0
    while off < total:
        out.append((off, min(w, total - off)))
        off += w
    return out


def cast_blk(cx, block, src_ap, done_sem, gates=()):
    wb = cx.dtmp("wb", src_ap.shape, BF16)

    def mk(g):
        for s, v in gates:
            s.wait(g, v)
        done_sem.inc(g.dma_start(out=wb, in_=src_ap), 16)

    block.gpsimd(mk)
    return wb



def gemm(cx, blk, *, wblkb=None, OC, IC, K, wcol=None, wq="sync", rhs_fn,
         out_ts, banks, bank0, name, evac_s=None, evac_v=None, cross_dep=False,
         w_gates=(), rhs_gate_fn=None, te_gates=(), lhsT_fn=None, chain=None,
         o0=0, wsel=None):
    NT = len(out_ts)
    assert wsel is None or OC == 1

    def wi(o):
        return o % 2 if wsel is None else wsel
    if chain is not None:
        tt = cx.sem_on(chain["sems"][2])
        ev = cx.sem_on(chain["sems"][3])
    else:
        tt = cx.sem("tt")
        ev = cx.sem("ev")
    wl = None
    if lhsT_fn is None:
        if wq == "gpsimd":
            wl = cx.sems(OC, "wlg", fresh=True)
        elif chain is not None and chain["sems"][0] is not None:
            wl = [cx.sem_on(chain["sems"][0]), cx.sem_on(chain["sems"][1])]
        else:
            wl = cx.sems(2, "wl")

        def mk_w(s):
            for g, v in w_gates:
                g.wait(s, v)
            for o in range(OC):
                if o >= 2:
                    tt.wait(s, (o - 1) * IC)
                dst = wl[o] if wq == "gpsimd" else wl[wi(o)]
                dst.inc(
                    s.dma_start(out=wcol[wi(o)][0:K, 0:IC * 128],
                                in_=wblkb[o0 + o].rearrange(
                                    "k ic of -> k (ic of)")), 16)
        getattr(blk, wq)(mk_w)

    def mk_te(te):
        for g, v in te_gates:
            g.wait(te, v)
        if lhsT_fn is not None:
            for g, v in w_gates:
                g.wait(te, v)
        for o in range(OC):
            if lhsT_fn is None:
                if wq == "gpsimd":
                    wl[o].wait(te, 16)
                else:
                    wl[wi(o)].wait(te,
                                   16 * ((o // 2 if wsel is None else 0) + 1))
            if o >= 2:
                ev.wait(te, (o - 1) * NT)
            for i in range(IC):
                if rhs_gate_fn is not None:
                    g = rhs_gate_fn(i)
                    if g is not None:
                        g[0].wait(te, g[1])
                rhs = rhs_fn(i)
                lhsT = (lhsT_fn(o, i) if lhsT_fn is not None
                        else wcol[wi(o)][0:K, 128 * i:128 * (i + 1)])
                for ti, (toff, tw) in enumerate(out_ts):
                    mm = te.matmul(
                        banks[bank0 + (o % 2) * NT + ti][:, 0:tw],
                        lhsT,
                        rhs[:, toff:toff + tw],
                        start=(i == 0), stop=(i == IC - 1))
            tt.inc(mm, IC)

    def mk_ev(engname, fn):
        def mk(eng):
            for o in range(OC):
                for ti, (toff, tw) in enumerate(out_ts):
                    if fn(None, o, ti, None) != "mine":
                        continue
                    ps = banks[bank0 + (o % 2) * NT + ti][:, 0:tw]
                    tt.wait(eng, o * IC + IC)
                    if cross_dep and engname == "vector":
                        ev.wait(eng, o * NT)
                    ev.inc(fn(eng, o, ti, ps), 1)
        return mk

    blk.tensor(mk_te)
    if evac_s is not None:
        blk.scalar(mk_ev("scalar", evac_s))
    if evac_v is not None:
        blk.vector(mk_ev("vector", evac_v))
    return dict(ev=ev, ev_total=OC * NT, tt=tt, tt_total=OC * IC,
                sems=(wl[0] if wl else None, wl[1] if wl else None, tt, ev))



def ln_stats_stream(cx, blk, dram, nC, nt, banks, bank0, rowstage, srow,
                    xst, xld, gates=(), tag="st", ld_base=0):
    ones = cx.sb(f"ones_{tag}", [128, 1], F32)
    sq = cx.sb(f"sq_{tag}", [128, 512], F32)
    ssem = cx.sem("sq")
    qcons = cx.sem("qc")
    vdone = cx.sem("vd")
    done = cx.sem("sd")
    vs = cx.sem("vo")
    ts = tslices(nt)
    NTS = len(ts)

    def mk_ld(s):
        for g, v in gates:
            g.wait(s, v)
        for i in range(nC):
            if ld_base + i >= 2:
                qcons.wait(s, (i - 1) * NTS)
            xld[i % 2].inc(
                s.dma_start(out=xst[i % 2][:, 0:nt],
                            in_=dram[128 * i:128 * (i + 1), :]), 16)
        vdone.wait(s, 1)
        done.inc(s.dma_start(out=srow[0:1, :], in_=rowstage[0:1, 0:nt]), 16)
        vdone.wait(s, 2)
        done.inc(s.dma_start(out=srow[1:2, :], in_=rowstage[0:1, 0:nt]), 16)
        done.wait(s, 32)

    def mk_s(sc):
        n = 0
        for i in range(nC):
            xld[i % 2].wait(sc, 16 * ((ld_base + i) // 2 + 1))
            for ti, (toff, tw) in enumerate(ts):
                if n >= 1:
                    qcons.wait(sc, n)
                ssem.inc(
                    sc.activation(out=sq[:, 0:tw],
                                  in_=xst[i % 2][:, toff:toff + tw],
                                  func=AF.Square), 1)
                n += 1

    def mk_te(te):
        vs.wait(te, 1)
        n = 0
        for i in range(nC):
            xld[i % 2].wait(te, 16 * ((ld_base + i) // 2 + 1))
            for ti, (toff, tw) in enumerate(ts):
                te.matmul(banks[bank0 + ti][0:1, 0:tw], ones[:, 0:1],
                          xst[i % 2][:, toff:toff + tw],
                          start=(i == 0), stop=(i == nC - 1))
                ssem.wait(te, n + 1)
                mm = te.matmul(banks[bank0 + 4 + ti][0:1, 0:tw], ones[:, 0:1],
                               sq[:, 0:tw], start=(i == 0), stop=(i == nC - 1))
                qcons.inc(mm, 1)
                n += 1

    def mk_v(v):
        vs.inc(v.memset(ones[:], 1.0), 1)
        qcons.wait(v, nC * NTS)
        for ti, (toff, tw) in enumerate(ts):
            last = v.tensor_copy(out=rowstage[0:1, toff:toff + tw],
                                 in_=banks[bank0 + ti][0:1, 0:tw])
        vdone.inc(last, 1)
        done.wait(v, 16)
        for ti, (toff, tw) in enumerate(ts):
            last = v.tensor_copy(out=rowstage[0:1, toff:toff + tw],
                                 in_=banks[bank0 + 4 + ti][0:1, 0:tw])
        vdone.inc(last, 1)

    blk.sync(mk_ld)
    blk.scalar(mk_s)
    blk.tensor(mk_te)
    blk.vector(mk_v)
    return done


def ln_rows(cx, blk, srow, prow, rowsq, nt, gates, tag="lr",
            inner_g=False, nfeat=D):
    Q = nt // 128
    sx = rowsq[:, 0 * Q:1 * Q]
    sq_ = rowsq[:, 1 * Q:2 * Q]
    m = rowsq[:, 2 * Q:3 * Q]
    r = rowsq[:, 3 * Q:4 * Q]
    mr = rowsq[:, 4 * Q:5 * Q]
    rho = rowsq[:, 5 * Q:6 * Q]
    sca = rowsq[:, 6 * Q:7 * Q]
    scb = rowsq[:, 7 * Q:8 * Q]
    veps = rowsq[:, 8 * Q:9 * Q]
    tmp = rowsq[:, 9 * Q:10 * Q]
    done = cx.sem("lrd")
    lds = cx.sem("lrl")
    vs = cx.sem("lrv")
    ser_v = cx.sem("lrserv")
    ser_s = cx.sem("lrsers")

    def mkch(ser):
        def ch(eng, instr):
            ser.inc(instr, 1)
            ser.wait(eng, ser.n)
            return instr
        return ch

    def sqv(row_dram):
        return row_dram.rearrange("one (p c) -> one p c", p=128).squeeze(0)

    def newton(v, ch, y, ve):
        for _ in range(2):
            ch(v, v.tensor_tensor(out=tmp, in0=y, in1=y, op=ALU.mult))
            ch(v, v.tensor_tensor(out=tmp, in0=tmp, in1=ve, op=ALU.mult))
            ch(v, v.tensor_scalar_mul(out=tmp, in0=tmp, scalar1=-0.5))
            ch(v, v.tensor_scalar_add(out=tmp, in0=tmp, scalar1=1.5))
            last = ch(v, v.tensor_tensor(out=y, in0=y, in1=tmp, op=ALU.mult))
        return last

    def mk_ld(s):
        for g, v in gates:
            g.wait(s, v)
        lds.inc(s.dma_start(out=sx, in_=sqv(srow[0:1, :])), 16)
        lds.inc(s.dma_start(out=sq_, in_=sqv(srow[1:2, :])), 16)
        vs.wait(s, 5 if inner_g else 3)
        done.inc(s.dma_start(out=sqv(prow[0:1, :]), in_=m), 16)
        done.inc(s.dma_start(out=sqv(prow[1:2, :]), in_=r), 16)
        done.inc(s.dma_start(out=sqv(prow[2:3, :]), in_=mr), 16)
        if inner_g:
            done.inc(s.dma_start(out=sqv(prow[3:4, :]), in_=rho), 16)
        done.wait(s, 16 * (4 if inner_g else 3))

    def mk_v(v):
        ch = mkch(ser_v)
        lds.wait(v, 32)
        ch(v, v.tensor_scalar_mul(out=m, in0=sx, scalar1=1.0 / nfeat))
        ch(v, v.tensor_scalar_mul(out=sca, in0=sq_, scalar1=1.0 / nfeat))
        ch(v, v.tensor_tensor(out=scb, in0=m, in1=m, op=ALU.mult))
        ch(v, v.tensor_tensor(out=sca, in0=sca, in1=scb, op=ALU.subtract))
        vs.inc(v.tensor_scalar_add(out=veps, in0=sca, scalar1=EPS), 1)
        vs.wait(v, 2)
        newton(v, ch, r, veps)
        mr_op = v.tensor_tensor(out=mr, in0=m, in1=r, op=ALU.mult)
        if not inner_g:
            vs.inc(mr_op, 1)
        else:
            ch(v, v.tensor_tensor(out=sca, in0=sca, in1=r, op=ALU.mult))
            ch(v, v.tensor_tensor(out=sca, in0=sca, in1=r, op=ALU.mult))
            vs.inc(v.tensor_scalar_add(out=scb, in0=sca, scalar1=EPS), 1)
            vs.wait(v, 4)
            newton(v, ch, rho, scb)
            vs.inc(v.tensor_tensor(out=rho, in0=rho, in1=r, op=ALU.mult), 1)

    def mk_s(sc):
        ch = mkch(ser_s)
        vs.wait(sc, 1)
        ch(sc, sc.activation(out=tmp, in_=veps, func=AF.Ln))
        vs.inc(sc.activation(out=r, in_=tmp, func=AF.Exp, scale=-0.5), 1)
        if inner_g:
            vs.wait(sc, 3)
            ch(sc, sc.activation(out=tmp, in_=scb, func=AF.Ln))
            vs.inc(sc.activation(out=rho, in_=tmp, func=AF.Exp, scale=-0.5), 1)

    blk.sync(mk_ld)
    blk.vector(mk_v)
    blk.scalar(mk_s)
    return done, 16 * (4 if inner_g else 3)


def pe_bcast(cx, blk, row_dram, nt, bank, dst, rowstage, gates, tag="bc",
             ones_t=None):
    onesr = ones_t if ones_t is not None else cx.sb(f"or_{tag}", [1, 128], F32)
    done = cx.sem("bcd")
    vs = cx.sem("bcv")
    lds = cx.sem("bcl")
    mdone = cx.sem("bcm")
    ts = tslices(nt)

    def mk_ld(s):
        for g, val in gates:
            g.wait(s, val)
        lds.inc(s.dma_start(out=rowstage[0:1, 0:nt], in_=row_dram), 16)

    def mk_v(v):
        vs.inc(v.memset(onesr[:], 1.0), 1)
        for ti, (toff, tw) in enumerate(ts):
            mdone.wait(v, ti + 1)
            done.inc(v.tensor_copy(out=dst[:, toff:toff + tw],
                                   in_=bank[:, 0:tw]), 1)

    def mk_te(te):
        vs.wait(te, 1)
        lds.wait(te, 16)
        for ti, (toff, tw) in enumerate(ts):
            if ti >= 1:
                done.wait(te, ti)
            mdone.inc(te.matmul(bank[:, 0:tw], onesr[0:1, :],
                                rowstage[0:1, toff:toff + tw],
                                start=True, stop=True), 1)

    blk.sync(mk_ld)
    blk.vector(mk_v)
    blk.tensor(mk_te)
    return done, len(ts)



def router_block(cx, blk, Xi, prow, wg_sb, tri_sb, id_sb, rowstage, banks,
                 rk_o, gt_o, mk_o, cn_o):
    NG = 16
    xg = [cx.sb("rxg", [128, DC * 128], F32) for _ in range(2)]
    mrcols = cx.sb("rmr", [128, 32], F32)
    csbc = cx.sb("rcs", [128, 8], F32)
    t8a = cx.sb("rt8a", [128, 8], F32)
    lg = cx.sb("rlg", [128, 8], F32)
    et = cx.sb("ret", [128, 8], F32)
    mask_sb = cx.sb("rmask", [128, 8], F32)
    cum_sb = cx.sb("rcum", [128, 8], F32)
    cols = cx.sb("rcols", [128, 4], F32)
    cs_sb = cx.sb("rcsr", [1, 8], F32)
    tot_sb = cx.sb("rtot", [1, 8], F32)
    offs_sb = cx.sb("roff", [1, 8], F32)
    onesc = cx.sb("ronec", [128, 1], F32)
    onesr = cx.sb("roner", [1, 128], F32)
    rankcols = cx.sb("rrank", [128, NG], F32)
    gatecols = cx.sb("rgate", [128, NG], F32)
    maskstack = cx.sb("rmsk", [128, NG * 8], F32)
    cx.dbg = dict(lg=lg, et=et, mask=mask_sb, csbc=csbc, mrcols=mrcols,
                  offs=offs_sb, cum=cum_sb, cols=cols, xg=xg[0], wg=wg_sb)

    rxl = cx.sems(2, "rxl", pinned=True)
    rsl = cx.sem("rsl", pinned=True)
    rpre = cx.sem("rpre", pinned=True)
    rt = cx.sem("rt", pinned=True)
    rv = cx.sem("rv", pinned=True)
    rvm = cx.sem("rvm", pinned=True)
    rvo = cx.sem("rvo", pinned=True)
    rvx = cx.sem("rvx", pinned=True)
    rsx = cx.sem("rsx", pinned=True)
    rvp = cx.sem("rvp", pinned=True)
    rst = cx.sem("rst", pinned=True)
    ser = cx.sem("rser", pinned=True)
    sers = cx.sem("rsers", pinned=True)

    def ch(v, instr):
        ser.inc(instr, 1)
        ser.wait(v, ser.n)
        return instr

    def mk_ld(s):
        rsl.inc(s.dma_start(out=rowstage[0:1, :], in_=prow[0:1, :]), 16)
        rpre.wait(s, NG)
        rsl.inc(s.dma_start(out=rowstage[0:1, :], in_=prow[1:2, :]), 16)
        for g in range(NG):
            if g >= 2:
                rt.wait(s, 3 * (g - 1))
            rxl[g % 2].inc(
                s.dma_start(out=xg[g % 2][:],
                            in_=Xi.rearrange("(ic p) t -> p ic t", p=128)
                            [:, :, 128 * g:128 * (g + 1)]), 16)
        rv.wait(s, NG)
        rvo.wait(s, NG)
        rst.inc(s.dma_start(out=rk_o, in_=rankcols[:]), 16)
        rst.inc(s.dma_start(out=gt_o, in_=gatecols[:]), 16)
        rst.inc(s.dma_start(out=mk_o, in_=maskstack[:]), 16)
        rst.inc(s.dma_start(out=cn_o, in_=offs_sb[:]), 16)
        rst.wait(s, 64)
    blk.sync(mk_ld)

    def mk_prep_v(v):
        ch(v, v.memset(onesc[:], 1.0))
        ch(v, v.memset(onesr[:], 1.0))
        ch(v, v.memset(offs_sb[:], 0.0))
        rvp.inc(v.memset(cols[:], 0.0), 1)
        rpre.wait(v, 2 * NG)
        rvp.inc(v.tensor_copy(out=mrcols[:], in_=banks[4][:, 0:32]), 1)
        rpre.wait(v, 2 * NG + 1)
        rvp.inc(v.tensor_copy(out=cs_sb[:], in_=banks[2][0:1, 0:8]), 1)
        rpre.wait(v, 2 * NG + 2)
        rvp.inc(v.tensor_copy(out=csbc[:], in_=banks[2][:, 0:8]), 1)
    blk.vector(mk_prep_v)

    def mk_pe(te):
        rsl.wait(te, 16)
        for g in range(NG):
            rpre.inc(te.transpose(banks[4][:, g:g + 1],
                                  rowstage[0:1, 128 * g:128 * (g + 1)],
                                  id_sb[0:1, 0:1]), 1)
        rsl.wait(te, 32)
        for g in range(NG):
            rpre.inc(te.transpose(banks[4][:, 16 + g:17 + g],
                                  rowstage[0:1, 128 * g:128 * (g + 1)],
                                  id_sb[0:1, 0:1]), 1)
        rvp.wait(te, 1)
        for ic in range(DC):
            mm = te.matmul(banks[2][0:1, 0:8], onesc[:, 0:1],
                           wg_sb[:, 8 * ic:8 * (ic + 1)],
                           start=(ic == 0), stop=(ic == DC - 1))
        rpre.inc(mm, 1)
        rvp.wait(te, 3)
        rpre.inc(te.matmul(banks[2][:, 0:8], onesr[0:1, :], cs_sb[0:1, :],
                           start=True, stop=True), 1)
        for g in range(NG):
            rxl[g % 2].wait(te, 16 * (g // 2 + 1))
            if g >= 1:
                rv.wait(te, g)
            for ic in range(DC):
                mm = te.matmul(banks[0][:, 0:8],
                               xg[g % 2][:, 128 * ic:128 * (ic + 1)],
                               wg_sb[:, 8 * ic:8 * (ic + 1)],
                               start=(ic == 0), stop=(ic == DC - 1))
            rt.inc(mm, 1)
            rvm.wait(te, g + 1)
            if g == 0:
                rvp.wait(te, 4)
            te.matmul(banks[1][:, 0:8], tri_sb[:], mask_sb[:],
                      start=True, stop=True)
            rt.inc(te.matmul(banks[2][0:1, 0:8], onesc[:, 0:1], mask_sb[:],
                             start=True, stop=True), 1)
            rvo.wait(te, g)
            rt.inc(te.matmul(banks[3][:, 0:8], onesr[0:1, :], offs_sb[0:1, :],
                             start=True, stop=True), 1)
    blk.tensor(mk_pe)

    def mk_rv(v):
        rvp.wait(v, 4)
        for g in range(NG):
            rt.wait(v, 3 * g + 1)
            ch(v, v.tensor_scalar_mul(out=t8a[:], in0=csbc[:],
                                      scalar1=mrcols[:, g:g + 1]))
            ch(v, v.tensor_tensor(out=lg[:], in0=banks[0][:, 0:8], in1=t8a[:],
                                  op=ALU.subtract))
            ch(v, v.tensor_scalar_mul(out=lg[:], in0=lg[:],
                                      scalar1=mrcols[:, 16 + g:17 + g]))
            ch(v, v.reduce_max(out=cols[:, 0:1], in_=lg[:], axis=AX.X))
            rvx.inc(v.tensor_scalar_mul(out=cols[:, 1:2], in0=cols[:, 0:1],
                                        scalar1=-1.0), 1)
            rsx.wait(v, g + 1)
            ch(v, v.reduce_sum(out=cols[:, 2:3], in_=et[:], axis=AX.X))
            ch(v, v.reciprocal(out=cols[:, 3:4], in_=cols[:, 2:3]))
            ch(v, v.tensor_scalar_mul(out=et[:], in0=et[:],
                                      scalar1=cols[:, 3:4]))
            ch(v, v.reduce_max(out=cols[:, 0:1], in_=et[:], axis=AX.X))
            ch(v, v.tensor_copy(out=gatecols[:, g:g + 1], in_=cols[:, 0:1]))
            ch(v, v.tensor_single_scalar(out=mask_sb[:], in_=et[:],
                                         scalar=cols[:, 0:1], op=ALU.is_ge))
            rvm.inc(v.tensor_copy(out=maskstack[:, 8 * g:8 * (g + 1)],
                                  in_=mask_sb[:]), 1)
            rt.wait(v, 3 * g + 3)
            ch(v, v.tensor_copy(out=cum_sb[:], in_=banks[1][:, 0:8]))
            ch(v, v.tensor_tensor(out=cum_sb[:], in0=cum_sb[:],
                                  in1=banks[3][:, 0:8], op=ALU.add))
            ch(v, v.tensor_tensor(out=cum_sb[:], in0=cum_sb[:], in1=mask_sb[:],
                                  op=ALU.mult))
            ch(v, v.reduce_sum(out=cols[:, 1:2], in_=cum_sb[:], axis=AX.X))
            ch(v, v.tensor_copy(out=rankcols[:, g:g + 1], in_=cols[:, 1:2]))
            ch(v, v.tensor_copy(out=tot_sb[:], in_=banks[2][0:1, 0:8]))
            rvo.inc(v.tensor_tensor(out=offs_sb[:], in0=offs_sb[:],
                                    in1=tot_sb[:], op=ALU.add), 1)
            rv.inc(v.tensor_copy(out=cols[:, 2:3], in_=cols[:, 1:2]), 1)
    blk.vector(mk_rv)

    def mk_rs(sc):
        for g in range(NG):
            rvx.wait(sc, g + 1)
            rsx.inc(sc.activation(out=et[:], in_=lg[:], func=AF.Exp,
                                  bias=cols[:, 1:2]), 1)
    blk.scalar(mk_rs)


def build_L1a():
    return build_ffn0(inner_prep=True, router=False)


def build_ffn0(inner_prep, router):
    cx = Cx("ffn0")
    X0 = cx.din("X0", [D, T])
    w12 = cx.din("w12", [64, 128, DC, 128])
    h1 = cx.din("h1", [HC, 128, HC, 128])
    h2 = cx.din("h2", [HC, 128, HC, 128])
    w3 = cx.din("w3", [DC, 128, HC, 128])
    s1 = cx.din("s1", [128, HC]); s2 = cx.din("s2", [128, HC])
    sh1 = cx.din("sh1", [128, HC]); sh2 = cx.din("sh2", [128, HC])
    s3 = cx.din("s3", [128, DC])
    if inner_prep:
        g1 = cx.din("g1", [128, DC]); b1 = cx.din("b1", [128, DC])
        XNAo = cx.dout("XNA", [D, T], BF16)
        st1o = cx.dout("st1", [4, T])
    if router:
        wg_in = cx.din("wg", [128, 8 * DC])
        tri_in = cx.din("tri", [128, 128])
        id_in = cx.din("id128", [128, 128])
        XLo = cx.dout("XL3", [D, T], BF16)
        rk_o = cx.dout("rank", [128, 16])
        gt_o = cx.dout("gateo", [128, 16])
        mk_o = cx.dout("masko", [128, 128])
        cn_o = cx.dout("counts", [1, 8])
    X1o = cx.dout("X1", [D, T])
    X1i = cx.dtmp("X1i", [D, T], F32)

    TH = T // 4
    with cx.st:
        blk = cx.block()
        XL = [cx.sb("XL", [128, T], BF16) for _ in range(DC)]
        hbuf = [cx.sb("hb", [128, TH], BF16) for _ in range(HC)]
        abuf = [cx.sb("ab", [128, TH], BF16) for _ in range(HC)]
        sa = cx.sb("sa", [128, 512], F32)
        scv = cx.sb("scv", [128, HC * 4 + DC * 3], F32)
        mbc = cx.sb("mbc", [128, T], F32)
        rbc = cx.sb("rbc", [128, T], F32)
        xscr = cx.sb("xscr", [128, 512], F32)
        wcol = [cx.sb("wcol", [128, HC * 128], BF16) for _ in range(2)]
        rowsq = cx.sb("rowsq", [128, 10 * (T // 128)], F32)
        rowstage = cx.sb("rowstage", [1, T], F32)
        srow = cx.dtmp("srow", [2, T], F32)
        prow0 = cx.dtmp("prow0", [4, T], F32)
        prow1 = cx.dtmp("prow1", [4, T], F32)
        xst = [cx.sb("xst", [128, T], F32) for _ in range(2)]
        Xq = [cx.sb("Xq", [128, 512], F32) for _ in range(DC)]
        xna = [cx.sb("xna", [128, 512], BF16) for _ in range(2)]
        banks = [cx.ps() for _ in range(8)]
        ld = cx.sem("ld", pinned=True)
        xld = cx.sems(2, "xld", pinned=True)

        c_s1 = scv[:, 0:HC]
        c_s2 = scv[:, HC:2 * HC]
        c_h1 = scv[:, 2 * HC:3 * HC]
        c_h2 = scv[:, 3 * HC:4 * HC]
        c_s3h = scv[:, 4 * HC:4 * HC + DC]
        c_g1 = scv[:, 4 * HC + DC:4 * HC + 2 * DC]
        c_b1 = scv[:, 4 * HC + 2 * DC:4 * HC + 3 * DC]

        pairs = [(c_s1, s1), (c_s2, s2), (c_h1, sh1), (c_h2, sh2), (c_s3h, s3)]
        if inner_prep:
            pairs += [(c_g1, g1), (c_b1, b1)]
        if router:
            wg_sb = cx.sb("wg", [128, 8 * DC], F32)
            tri_sb = cx.sb("tri", [128, 128], F32)
            id_sb = cx.sb("id", [128, 128], F32)
            pairs += [(wg_sb[:, :], wg_in), (tri_sb[:, :], tri_in),
                      (id_sb[:, :], id_in)]

        def mk_ld(s):
            for dst, srcw in pairs:
                ld.inc(s.dma_start(out=dst, in_=srcw), 16)
        blk.sync(mk_ld)
        LD_ALL = 16 * len(pairs)

        s3r = cx.sem("s3r", pinned=True)

        def mk_s3(v):
            ld.wait(v, LD_ALL)
            s3r.inc(v.tensor_scalar_mul(out=c_s3h, in0=c_s3h, scalar1=0.5), 1)
        blk.vector(mk_s3)

        csem = cx.sems(4, "cast", pinned=True)
        w12b = cast_blk(cx, blk, w12, csem[0])
        h1b = cast_blk(cx, blk, h1, csem[1])
        h2b = cast_blk(cx, blk, h2, csem[2])
        w3b = cast_blk(cx, blk, w3, csem[3])

        sdone = ln_stats_stream(cx, blk, X0, DC, T, banks, 0, rowstage, srow,
                                xst, xld, tag="ln0")
        rdone, rdn = ln_rows(cx, blk, srow, prow0, rowsq, T, gates=[(sdone, 32)],
                             tag="ln0")
        d1, n1 = pe_bcast(cx, blk, prow0[0:1, :], T, banks[0], mbc, rowstage,
                          gates=[(rdone, rdn)], tag="m0")
        d2, n2 = pe_bcast(cx, blk, prow0[1:2, :], T, banks[1], rbc, rowstage,
                          gates=[(rdone, rdn), (d1, n1)], tag="r0")

        xls = cx.sem("xls", pinned=True)
        xcons = cx.sem("xc")

        def mk_ld2(s):
            d1.wait(s, n1)
            d2.wait(s, n2)
            for i in range(DC):
                if i >= 2:
                    xcons.wait(s, i - 1)
                xld[i % 2].inc(
                    s.dma_start(out=xst[i % 2][:],
                                in_=X0[128 * i:128 * (i + 1), :]), 16)
        blk.sync(mk_ld2)

        def mk_xl(v):
            for i in range(DC):
                xld[i % 2].wait(v, 16 * ((DC + i) // 2 + 1))
                ts4 = tslices(T)
                for ci, (toff, tw) in enumerate(ts4):
                    sub = v.tensor_tensor(out=xscr[:, 0:tw],
                                          in0=xst[i % 2][:, toff:toff + tw],
                                          in1=mbc[:, toff:toff + tw],
                                          op=ALU.subtract)
                    if ci == len(ts4) - 1:
                        xcons.inc(sub, 1)
                    mul = v.tensor_tensor(out=XL[i][:, toff:toff + tw],
                                          in0=xscr[:, 0:tw],
                                          in1=rbc[:, toff:toff + tw],
                                          op=ALU.mult)
                    if ci == len(ts4) - 1:
                        xls.inc(mul, 1)
        blk.vector(mk_xl)

        barrier(cx, blk, "p1")
        cx.checkpoint()

        prev_tt = None
        qst = cx.sems(2, "qst", pinned=True)
        qld = cx.sems(2, "qld", pinned=True)
        nq = [0]
        for half in range(4):
            hoff = half * TH
            out_ts = [(0, TH)]

            def ev_s_g12(sc, o, ti, ps):
                if o % 2 != 0:
                    return None
                if sc is None:
                    return "mine"
                k = o // 2
                return sc.activation(out=sa[:], in_=ps, func=AF.Silu,
                                     scale=c_s1[:, k:k + 1])

            def ev_v_g12(v, o, ti, ps):
                if o % 2 != 1:
                    return None
                if v is None:
                    return "mine"
                k = o // 2
                return v.scalar_tensor_tensor(
                    out=hbuf[k][:], in0=ps, scalar=c_s2[:, k:k + 1], in1=sa[:],
                    op0=ALU.mult, op1=ALU.mult)

            wg = [(csem[0], 16), (s3r, 1)]
            if prev_tt is not None:
                wg.append(prev_tt)
            g12 = gemm(cx, blk, wblkb=w12b, OC=64, IC=DC, K=128, wcol=wcol,
                       wq="sync",
                       rhs_fn=lambda i, hoff=hoff: XL[i][:, hoff:hoff + TH],
                       out_ts=out_ts, banks=banks, bank0=0, name=f"g12_{half}",
                       evac_s=ev_s_g12, evac_v=ev_v_g12, cross_dep=True,
                       w_gates=wg, te_gates=[(xls, DC)])

            def ev_s_h1(sc, o, ti, ps):
                if sc is None:
                    return "mine"
                return sc.activation(out=abuf[o][:], in_=ps, func=AF.Gelu,
                                     scale=c_h1[:, o:o + 1])

            gh1 = gemm(cx, blk, wblkb=h1b, OC=HC, IC=HC, K=128, wcol=wcol,
                       wq="sync", rhs_fn=lambda i: hbuf[i][:],
                       out_ts=out_ts, banks=banks, bank0=4, name=f"h1_{half}",
                       evac_s=ev_s_h1,
                       w_gates=[(csem[1], 16), (g12["tt"], g12["tt_total"])],
                       rhs_gate_fn=lambda i, g=g12: (g["ev"], 2 * i + 2))

            def ev_s_h2(sc, o, ti, ps):
                if sc is None:
                    return "mine"
                return sc.activation(out=hbuf[o][:], in_=ps, func=AF.Gelu,
                                     scale=c_h2[:, o:o + 1])

            gh2 = gemm(cx, blk, wblkb=h2b, OC=HC, IC=HC, K=128, wcol=wcol,
                       wq="sync", rhs_fn=lambda i: abuf[i][:],
                       out_ts=out_ts, banks=banks, bank0=0, name=f"h2_{half}",
                       evac_s=ev_s_h2,
                       w_gates=[(csem[2], 16), (gh1["tt"], gh1["tt_total"])],
                       rhs_gate_fn=lambda i, g=gh1: (g["ev"], i + 1))

            q = half

            def mk_xq_ld(s, hoff=hoff, q=q):
                for o in range(DC):
                    if q >= 1:
                        qst[o % 2].wait(s, 32 * 4 * q)
                    qld[o % 2].inc(
                        s.dma_start(out=Xq[o][:],
                                    in_=X0[128 * o:128 * (o + 1),
                                           hoff:hoff + TH]), 16)
            blk.sync(mk_xq_ld)

            def ev_v_w3(v, o, ti, ps, q=q):
                if v is None:
                    return "mine"
                qld[o % 2].wait(v, 16 * (4 * q + o // 2 + 1))
                return v.scalar_tensor_tensor(
                    out=Xq[o][:], in0=ps, scalar=c_s3h[:, o:o + 1],
                    in1=Xq[o][:], op0=ALU.mult, op1=ALU.add)

            gw3 = gemm(cx, blk, wblkb=w3b, OC=DC, IC=HC, K=128, wcol=wcol,
                       wq="sync", rhs_fn=lambda i: hbuf[i][:],
                       out_ts=out_ts, banks=banks, bank0=4, name=f"w3_{half}",
                       evac_v=ev_v_w3,
                       w_gates=[(csem[3], 16), (gh2["tt"], gh2["tt_total"])],
                       rhs_gate_fn=lambda i, g=gh2: (g["ev"], i + 1))

            def mk_xq_st(s, hoff=hoff, g=gw3):
                for o in range(DC):
                    g["ev"].wait(s, o + 1)
                    qst[o % 2].inc(
                        s.dma_start(out=X1i[128 * o:128 * (o + 1),
                                            hoff:hoff + TH],
                                    in_=Xq[o][:]), 16)
                    qst[o % 2].inc(
                        s.dma_start(out=X1o[128 * o:128 * (o + 1),
                                            hoff:hoff + TH],
                                    in_=Xq[o][:]), 16)
            blk.sync(mk_xq_st)
            nq[0] += DC
            prev_tt = (gw3["tt"], gw3["tt_total"])

        def mk_qwait(s):
            qst[0].wait(s, 32 * 4 * 4)
            qst[1].wait(s, 32 * 4 * 4)
        blk.sync(mk_qwait)
        barrier(cx, blk, "p2")
        cx.checkpoint()

        sdone1 = ln_stats_stream(cx, blk, X1i, DC, T, banks, 0, rowstage, srow,
                                 xst, xld, tag="ln1", ld_base=2 * DC)
        rdone1, rdn1 = ln_rows(cx, blk, srow, prow1, rowsq, T,
                               gates=[(sdone1, 32)], tag="ln1",
                               inner_g=inner_prep)
        d3, n3 = pe_bcast(cx, blk, prow1[0:1, :], T, banks[0], mbc, rowstage,
                          gates=[(rdone1, rdn1)], tag="m1")
        rho_row = prow1[3:4, :] if inner_prep else prow1[1:2, :]
        d4, n4 = pe_bcast(cx, blk, rho_row, T, banks[1], rbc, rowstage,
                          gates=[(rdone1, rdn1), (d3, n3)], tag="rho1")

        xnas = cx.sems(2, "xnas")
        sts = cx.sems(2, "stx")
        stf = cx.sem("stf")
        xcons2 = cx.sem("xc2")

        xout = XNAo if inner_prep else XLo

        def mk_ld3_st(s):
            d3.wait(s, n3)
            d4.wait(s, n4)
            nch = 0
            for i in range(DC):
                if i >= 2:
                    xcons2.wait(s, i - 1)
                xld[i % 2].inc(
                    s.dma_start(out=xst[i % 2][:],
                                in_=X1i[128 * i:128 * (i + 1), :]), 16)
                for toff, tw in tslices(T):
                    xnas[nch % 2].wait(s, nch // 2 + 1)
                    sts[nch % 2].inc(
                        s.dma_start(out=xout[128 * i:128 * (i + 1),
                                             toff:toff + tw],
                                    in_=xna[nch % 2][:, 0:tw]), 16)
                    nch += 1
            if inner_prep:
                rdone1.wait(s, rdn1)
                stf.inc(s.dma_start(out=st1o, in_=prow1), 16)
                stf.wait(s, 16)
            sts[0].wait(s, 16 * (nch // 2))
            sts[1].wait(s, 16 * (nch // 2))
        blk.sync(mk_ld3_st)

        def mk_xna(v):
            nch = 0
            for i in range(DC):
                xld[i % 2].wait(v, 16 * ((3 * DC + i) // 2 + 1))
                ts4 = tslices(T)
                for ci, (toff, tw) in enumerate(ts4):
                    if nch >= 2:
                        sts[nch % 2].wait(v, 16 * ((nch - 2) // 2 + 1))
                    sub = v.tensor_tensor(out=xscr[:, 0:tw],
                                          in0=xst[i % 2][:, toff:toff + tw],
                                          in1=mbc[:, toff:toff + tw],
                                          op=ALU.subtract)
                    if ci == len(ts4) - 1:
                        xcons2.inc(sub, 1)
                    if inner_prep:
                        v.tensor_tensor(out=xscr[:, 0:tw], in0=xscr[:, 0:tw],
                                        in1=rbc[:, toff:toff + tw], op=ALU.mult)
                        xnas[nch % 2].inc(
                            v.tensor_scalar(out=xna[nch % 2][:, 0:tw],
                                            in0=xscr[:, 0:tw],
                                            scalar1=c_g1[:, i:i + 1],
                                            scalar2=c_b1[:, i:i + 1],
                                            op0=ALU.mult, op1=ALU.add), 1)
                    else:
                        xnas[nch % 2].inc(
                            v.tensor_tensor(out=xna[nch % 2][:, 0:tw],
                                            in0=xscr[:, 0:tw],
                                            in1=rbc[:, toff:toff + tw],
                                            op=ALU.mult), 1)
                    nch += 1
        blk.vector(mk_xna)

        if router:
            barrier(cx, blk, "pR")
            cx.checkpoint()
            router_block(cx, blk, X1i, prow1, wg_sb, tri_sb, id_sb,
                         rowstage, banks, rk_o, gt_o, mk_o, cn_o)

    return cx



PI = float(np.pi)


def build_attn0():
    cx = Cx("L1b")
    X1 = cx.din("X1", [D, T])
    XNA = cx.din("XNA", [D, T + 1], BF16)
    st1 = cx.din("st1", [4, T])
    w_in = cx.din("w_in", [1, 128, DC, 128])
    w_outT = cx.din("w_outT", [128, D])
    G2 = cx.din("G2", [128, 128])
    w_gate = cx.din("w_gate", [DC, 128, DC, 128])
    w_mlp1 = cx.din("w_mlp1", [HC, 128, DC, 128])
    w_mlp2 = cx.din("w_mlp2", [DC, 128, HC, 128])
    ldec = cx.din("ldec", [64, 1])
    freq = cx.din("freq", [64, 1])
    b_gate = cx.din("b_gate", [128, DC])
    b_m1 = cx.din("b_m1", [128, HC])
    b_m2 = cx.din("b_m2", [128, DC])
    g2c = cx.din("g2c", [128, DC])
    b2c = cx.din("b2c", [128, DC])

    X2o = cx.dout("X2", [D, T])
    x1iD = cx.dtmp("x1i", [D, T], F32)
    srow = cx.dtmp("srow", [2, T], F32)
    prow = cx.dtmp("prow", [4, T], F32)

    TQ = T // 4
    TE = T // 8
    with cx.st:
        blk = cx.block()
        XNL = [cx.sb("XNL", [128, T + 1], BF16) for _ in range(DC)]
        beta = cx.sb("beta", [128, T + 1], BF16)
        bswap = cx.sb("bswap", [128, T + 1], BF16)
        betac = cx.sb("betac", [128, T // 64], F32)
        intra = cx.sb("intra", [128, T], F32)
        ccat = cx.sb("ccat", [128, T], BF16)
        W2sb = cx.sb("W2sb", [128, D], BF16)
        woutT = cx.sb("wouT", [128, D], BF16)
        G2sb = cx.sb("G2sb", [128, 128], BF16)
        m1buf = [cx.sb("m1b", [128, TE], BF16) for _ in range(HC)]
        x2n = [cx.sb("x2n", [128, T], BF16) for _ in range(DC)]
        bc0 = cx.sb("bc0", [128, T], F32)
        bc1 = cx.sb("bc1", [128, T], F32)
        Xq = [cx.sb("Xq", [128, TQ], F32) for _ in range(DC)]
        xiq = [cx.sb("xiq", [128, TE], F32) for _ in range(DC)]
        x1q = [cx.sb("x1q", [128, TQ], F32) for _ in range(2)]
        gscr = [cx.sb("gscr", [128, TQ], F32) for _ in range(2)]
        xscr = cx.sb("xscr", [128, TQ], F32)
        xscr2 = cx.sb("xscr2", [128, TQ], F32)
        wcol = [cx.sb("wcol", [128, HC * 128], BF16) for _ in range(2)]
        rowsq = cx.sb("rowsq", [128, 10 * (T // 128)], F32)
        rowstage = cx.sb("rowstage", [1, T], F32)
        xst = [cx.sb("xst", [128, T], F32) for _ in range(2)]
        kio = cx.sb("kio", [64, 64], F32)
        kr2 = cx.sb("kr2", [128, 64], F32)
        kis = cx.sb("kis", [128, 64], F32)
        ki32 = cx.sb("ki32", [64, 64], I32)
        kfr = cx.sb("kfr", [64, 64], F32)
        kpsi = cx.sb("kpsi", [64, 64], F32)
        ksin = cx.sb("ksin", [64, 64], F32)
        kcos = cx.sb("kcos", [64, 64], F32)
        kmp = cx.sb("kmp", [64, 64], F32)
        kabs = cx.sb("kabs", [64, 64], F32)
        ksc = cx.sb("ksc", [64, 4], F32)
        scv = cx.sb("scv", [128, 3 * DC + HC], F32)
        banks = [cx.ps() for _ in range(8)]
        ld = cx.sem("ld", pinned=True)
        xld = cx.sems(2, "xld", pinned=True)

        c_bg = scv[:, 0:DC]
        c_b1 = scv[:, DC:DC + HC]
        c_b2 = scv[:, DC + HC:2 * DC + HC]
        c_g2 = scv[:, 2 * DC + HC:3 * DC + HC]

        csem = cx.sems(6, "cast", pinned=True)
        w_inb = cast_blk(cx, blk, w_in, csem[0])
        w_gateb = cast_blk(cx, blk, w_gate, csem[1])
        w_m1b = cast_blk(cx, blk, w_mlp1, csem[2])
        w_m2b = cast_blk(cx, blk, w_mlp2, csem[3])

        def mk_gcast(g):
            csem[4].inc(g.dma_start(out=woutT[:], in_=w_outT), 16)
            csem[5].inc(g.dma_start(out=G2sb[:], in_=G2), 16)
        blk.gpsimd(mk_gcast)

        def mk_ld(s):
            for i in range(DC):
                ld.inc(s.dma_start(out=XNL[i][:],
                                   in_=XNA[128 * i:128 * (i + 1), :]), 16)
            for dst, srcw in [(c_bg, b_gate), (c_b1, b_m1), (c_b2, b_m2),
                              (c_g2, g2c)]:
                ld.inc(s.dma_start(out=dst, in_=srcw), 16)
            ld.inc(s.dma_start(out=ksc[:, 0:1], in_=ldec), 16)
            ld.inc(s.dma_start(out=ksc[:, 1:2], in_=freq), 16)
        blk.sync(mk_ld)
        LD_ALL = 16 * (DC + 6)

        bb2 = cx.sb("bb2", [128, DC], F32)

        def mk_ld2(s):
            ld.inc(s.dma_start(out=bb2[:], in_=b2c), 16)
        blk.sync(mk_ld2)
        LD_ALL += 16

        ser_v = cx.sem("serv", pinned=True)
        ser_s = cx.sem("sers", pinned=True)

        def ch_v(v, instr):
            ser_v.inc(instr, 1)
            ser_v.wait(v, ser_v.n)
            return instr

        def ch_s(sc, instr):
            ser_s.inc(instr, 1)
            ser_s.wait(sc, ser_s.n)
            return instr

        kg = cx.sem("kg", pinned=True)
        v_psi = cx.sem("vpsi", pinned=True)
        s_mp = cx.sem("smp", pinned=True)
        s_sc = cx.sem("ssc", pinned=True)
        krdone = cx.sem("krdone", pinned=True)

        def mk_kg(g):
            kg.inc(g.iota(kio[:], pattern=[[1, 64]], base=0,
                          channel_multiplier=0,
                          allow_small_or_imprecise_dtypes=True), 1)
        blk.gpsimd(mk_kg)

        def mk_kv(v):
            ld.wait(v, LD_ALL)
            kg.wait(v, 1)
            ch_v(v, v.memset(ksc[:, 3:4], PI / 2))
            ch_v(v, v.tensor_scalar_mul(out=kpsi[:], in0=kio[:],
                                        scalar1=ksc[:, 1:2]))
            ch_v(v, v.tensor_scalar_mul(out=kpsi[:], in0=kpsi[:],
                                        scalar1=1.0 / (2 * PI)))
            ch_v(v, v.tensor_copy(out=ki32[:], in_=kpsi[:]))
            ch_v(v, v.tensor_copy(out=kfr[:], in_=ki32[:]))
            ch_v(v, v.tensor_tensor(out=kpsi[:], in0=kpsi[:], in1=kfr[:],
                                    op=ALU.subtract))
            ch_v(v, v.tensor_scalar_mul(out=kpsi[:], in0=kpsi[:],
                                        scalar1=2 * PI))
            ch_v(v, v.tensor_single_scalar(out=kfr[:], in_=kpsi[:], scalar=PI,
                                           op=ALU.is_gt))
            ch_v(v, v.scalar_tensor_tensor(out=kpsi[:], in0=kfr[:],
                                           scalar=-2 * PI, in1=kpsi[:],
                                           op0=ALU.mult, op1=ALU.add))
            ch_v(v, v.tensor_scalar_mul(out=kabs[:], in0=kpsi[:], scalar1=-1.0))
            v_psi.inc(v.tensor_tensor(out=kabs[:], in0=kpsi[:], in1=kabs[:],
                                      op=ALU.max), 1)
            s_mp.wait(v, 1)
            s_sc.wait(v, 1)
            ch_v(v, v.tensor_tensor(out=kr2[0:64, :], in0=kmp[:], in1=kcos[:],
                                    op=ALU.mult))
            ch_v(v, v.tensor_tensor(out=kis[64:128, :], in0=kmp[:], in1=ksin[:],
                                    op=ALU.mult))
            ch_v(v, v.tensor_copy(out=kr2[64:128, :], in_=kr2[0:64, :]))
            krdone.inc(v.tensor_scalar_mul(out=kis[0:64, :],
                                           in0=kis[64:128, :],
                                           scalar1=-1.0), 1)
        blk.vector(mk_kv)

        def mk_ks(sc):
            ld.wait(sc, LD_ALL)
            ch_s(sc, sc.activation(out=ksc[:, 2:3], in_=ksc[:, 0:1],
                                   func=AF.Sigmoid))
            ch_s(sc, sc.activation(out=ksc[:, 2:3], in_=ksc[:, 2:3],
                                   func=AF.Ln))
            kg.wait(sc, 1)
            s_mp.inc(sc.activation(out=kmp[:], in_=kio[:], func=AF.Exp,
                                   scale=ksc[:, 2:3]), 1)
            v_psi.wait(sc, 1)
            ch_s(sc, sc.activation(out=ksin[:], in_=kpsi[:], func=AF.Sin))
            s_sc.inc(sc.activation(out=kcos[:], in_=kabs[:], func=AF.Sin,
                                   scale=-1.0, bias=ksc[:, 3:4]), 1)
        blk.scalar(mk_ks)

        bts = tslices(T + 1)

        def ev_beta(v, o, ti, ps):
            if v is None:
                return "mine"
            toff, tw = bts[ti]
            return v.tensor_copy(out=beta[:, toff:toff + tw], in_=ps)

        gbeta = gemm(cx, blk, wblkb=w_inb, OC=1, IC=DC, K=128, wcol=wcol,
                     wq="sync", rhs_fn=lambda i: XNL[i][:],
                     out_ts=bts, banks=banks, bank0=0, name="beta",
                     evac_v=ev_beta,
                     w_gates=[(csem[0], 16)], te_gates=[(ld, LD_ALL)])

        w2m = cx.sem("w2m", pinned=True)
        w2c = cx.sem("w2c", pinned=True)

        def mk_w2_te(te):
            csem[4].wait(te, 16)
            csem[5].wait(te, 16)
            for half in range(2):
                w2m.inc(te.matmul(banks[5 + half][:, :],
                                  G2sb[:], woutT[:, 512 * half:512 * (half + 1)],
                                  start=True, stop=True), 1)
        blk.tensor(mk_w2_te)

        def mk_w2_v(v):
            for half in range(2):
                w2m.wait(v, half + 1)
                w2c.inc(v.tensor_copy(out=W2sb[:, 512 * half:512 * (half + 1)],
                                      in_=banks[5 + half][:, :]), 1)
        blk.vector(mk_w2_v)

        barrier(cx, blk, "pA")
        cx.checkpoint()

        eig = cx.sem("eig", pinned=True)

        def mk_eig(v):
            ch_v(v, v.tensor_copy(out=bswap[0:64, :], in_=beta[64:128, :]))
            ch_v(v, v.tensor_copy(out=bswap[64:128, :], in_=beta[0:64, :]))
            bv = beta[:, 1:T + 1].rearrange("p (c j) -> p c j", c=T // 64)
            sv = bswap[:, 1:T + 1].rearrange("p (c j) -> p c j", c=T // 64)
            iv = intra.rearrange("p (c j) -> p c j", c=T // 64)
            for p in range(64):
                if p == 0:
                    ch_v(v, v.tensor_scalar_mul(out=intra[:], in0=beta[:, 1:],
                                                scalar1=kr2[:, 0:1]))
                else:
                    ch_v(v, v.scalar_tensor_tensor(
                        out=iv[:, :, 0:64 - p], in0=bv[:, :, p:64],
                        scalar=kr2[:, p:p + 1], in1=iv[:, :, 0:64 - p],
                        op0=ALU.mult, op1=ALU.add))
                ch_v(v, v.scalar_tensor_tensor(
                    out=iv[:, :, 0:64 - p], in0=sv[:, :, p:64],
                    scalar=kis[:, p:p + 1], in1=iv[:, :, 0:64 - p],
                    op0=ALU.mult, op1=ALU.add))
            ch_v(v, v.tensor_copy(out=betac[:], in_=beta[:, 0:T:64]))
            ch_v(v, v.tensor_tensor(out=iv[:, :, 0:1],
                                    in0=betac[:].unsqueeze(2),
                                    in1=iv[:, :, 0:1], op=ALU.add))
            eig.inc(v.tensor_copy(out=ccat[:], in_=intra[:]), 1)

        def mk_eig_gate(v):
            gbeta["ev"].wait(v, gbeta["ev_total"])
            krdone.wait(v, 1)
        blk.vector(mk_eig_gate)
        blk.vector(mk_eig)

        d0, n0 = pe_bcast(cx, blk, st1[1:2, :], T, banks[0], bc0, rowstage,
                          gates=[(eig, 0)], tag="r1")
        d1b, n1b = pe_bcast(cx, blk, st1[2:3, :], T, banks[1], bc1, rowstage,
                            gates=[(d0, n0)], tag="mr1")

        barrier(cx, blk, "pB")
        cx.checkpoint()

        gtt = cx.sem("gtt", pinned=True)
        gs = cx.sem("gs", pinned=True)
        gv = cx.sem("gv", pinned=True)
        qld = cx.sems(DC, "qld", pinned=True)
        qst = cx.sems(2, "qst", pinned=True)
        wlg = cx.sems(2, "wlg", pinned=True)

        def mk_cw(s):
            csem[1].wait(s, 16)
            n = 0
            for q in range(4):
                for o in range(DC):
                    if n >= 2:
                        m = n - 2
                        gv.wait(s, m + 1)
                        qst[m % 2].inc(
                            s.dma_start(out=x1iD[128 * (m % DC):128 * (m % DC + 1),
                                                TQ * (m // DC):TQ * (m // DC + 1)],
                                        in_=x1q[m % 2][:]), 16)
                    if q >= 1:
                        qst[o % 2].wait(s, 16 * 4 * q)
                    qld[o].inc(
                        s.dma_start(out=Xq[o][:],
                                    in_=X1[128 * o:128 * (o + 1),
                                           TQ * q:TQ * (q + 1)]), 16)
                    if n >= 2:
                        gtt.wait(s, n - 1)
                    wlg[n % 2].inc(
                        s.dma_start(out=wcol[n % 2][:, 0:DC * 128],
                                    in_=w_gateb[o].rearrange(
                                        "k ic of -> k (ic of)")), 16)
                    n += 1
            for m in (30, 31):
                gv.wait(s, m + 1)
                qst[m % 2].inc(
                    s.dma_start(out=x1iD[128 * (m % DC):128 * (m % DC + 1),
                                        TQ * (m // DC):TQ * (m // DC + 1)],
                                in_=x1q[m % 2][:]), 16)
            qst[0].wait(s, 16 * 16)
            qst[1].wait(s, 16 * 16)
        blk.sync(mk_cw)

        def mk_cte(te):
            n = 0
            for q in range(4):
                for o in range(DC):
                    wlg[n % 2].wait(te, 16 * (n // 2 + 1))
                    if n >= 2:
                        gs.wait(te, n - 1)
                        gv.wait(te, n - 1)
                    for i in range(DC):
                        te.matmul(banks[n % 2][:, :], 
                                  wcol[n % 2][:, 128 * i:128 * (i + 1)],
                                  XNL[i][:, 1 + TQ * q:1 + TQ * (q + 1)],
                                  start=(i == 0), stop=(i == DC - 1))
                    mm = te.matmul(banks[2 + n % 2][:, :],
                                   W2sb[:, 128 * o:128 * (o + 1)],
                                   ccat[:, TQ * q:TQ * (q + 1)],
                                   start=True, stop=True)
                    gtt.inc(mm, 1)
                    n += 1
        blk.tensor(mk_cte)

        def mk_cev_s(sc):
            n = 0
            for q in range(4):
                for o in range(DC):
                    gtt.wait(sc, n + 1)
                    if n >= 2:
                        gv.wait(sc, n - 1)
                    gs.inc(sc.activation(out=gscr[n % 2][:],
                                         in_=banks[n % 2][:, :],
                                         func=AF.Sigmoid,
                                         bias=c_bg[:, o:o + 1]), 1)
                    n += 1
        blk.scalar(mk_cev_s)

        def mk_cev_v(v):
            n = 0
            nst = 0
            for q in range(4):
                for o in range(DC):
                    gs.wait(v, n + 1)
                    gtt.wait(v, n + 1)
                    qld[o].wait(v, 16 * (q + 1))
                    v.tensor_tensor(out=xscr[:], in0=Xq[o][:],
                                    in1=bc0[:, TQ * q:TQ * (q + 1)], op=ALU.mult)
                    v.tensor_tensor(out=xscr[:], in0=xscr[:],
                                    in1=bc1[:, TQ * q:TQ * (q + 1)],
                                    op=ALU.subtract)
                    v.tensor_tensor(out=xscr2[:], in0=gscr[n % 2][:],
                                    in1=banks[2 + n % 2][:, :], op=ALU.mult)
                    if nst >= 2:
                        qst[nst % 2].wait(v, 16 * ((nst - 2) // 2 + 1))
                    gv.inc(v.tensor_tensor(out=x1q[nst % 2][:], in0=xscr[:],
                                           in1=xscr2[:], op=ALU.add), 1)
                    n += 1
                    nst += 1
        blk.vector(mk_cev_v)

        barrier(cx, blk, "pC")
        cx.checkpoint()

        sdone = ln_stats_stream(cx, blk, x1iD, DC, T, banks, 0, rowstage, srow,
                                xst, xld, tag="ln2", ld_base=0)
        rdone, rdn = ln_rows(cx, blk, srow, prow, rowsq, T, gates=[(sdone, 32)],
                             tag="ln2")
        d2, n2 = pe_bcast(cx, blk, prow[0:1, :], T, banks[0], bc0, rowstage,
                          gates=[(rdone, rdn)], tag="m2")
        d3, n3 = pe_bcast(cx, blk, prow[1:2, :], T, banks[1], bc1, rowstage,
                          gates=[(rdone, rdn), (d2, n2)], tag="r2")

        x2s = cx.sem("x2s", pinned=True)
        xcons = cx.sem("xc")

        def mk_dld(s):
            d2.wait(s, n2)
            d3.wait(s, n3)
            for i in range(DC):
                if i >= 2:
                    xcons.wait(s, i - 1)
                xld[i % 2].inc(
                    s.dma_start(out=xst[i % 2][:],
                                in_=x1iD[128 * i:128 * (i + 1), :]), 16)
        blk.sync(mk_dld)

        def mk_dx2n(v):
            for i in range(DC):
                xld[i % 2].wait(v, 16 * ((DC + i) // 2 + 1))
                ts4 = tslices(T)
                for ci, (toff, tw) in enumerate(ts4):
                    sub = v.tensor_tensor(out=xscr[:, 0:tw],
                                          in0=xst[i % 2][:, toff:toff + tw],
                                          in1=bc0[:, toff:toff + tw],
                                          op=ALU.subtract)
                    if ci == len(ts4) - 1:
                        xcons.inc(sub, 1)
                    v.tensor_tensor(out=xscr[:, 0:tw], in0=xscr[:, 0:tw],
                                    in1=bc1[:, toff:toff + tw], op=ALU.mult)
                    last = v.tensor_scalar(out=x2n[i][:, toff:toff + tw],
                                           in0=xscr[:, 0:tw],
                                           scalar1=c_g2[:, i:i + 1],
                                           scalar2=bb2[:, i:i + 1],
                                           op0=ALU.mult, op1=ALU.add)
                x2s.inc(last, 1)
        blk.vector(mk_dx2n)

        barrier(cx, blk, "pD")
        cx.checkpoint()

        eld = cx.sems(DC, "eld", pinned=True)
        est = cx.sems(2, "est", pinned=True)

        prev = None
        for e in range(8):
            def mk_eld_e(s, e=e):
                for o in range(DC):
                    if e >= 1:
                        est[o % 2].wait(s, 16 * 4 * e)
                    eld[o].inc(
                        s.dma_start(out=Xq[o][:, 0:TE],
                                    in_=X1[128 * o:128 * (o + 1),
                                           TE * e:TE * (e + 1)]), 16)
                    eld[o].inc(
                        s.dma_start(out=xiq[o][:],
                                    in_=x1iD[128 * o:128 * (o + 1),
                                             TE * e:TE * (e + 1)]), 16)
            blk.sync(mk_eld_e)
            def ev_m1(sc, o, ti, ps, e=e):
                if sc is None:
                    return "mine"
                return sc.activation(out=m1buf[o][:], in_=ps, func=AF.Silu,
                                     bias=c_b1[:, o:o + 1])

            wg = [(csem[2], 16)]
            if prev is not None:
                wg.append(prev)
            gm1 = gemm(cx, blk, wblkb=w_m1b, OC=HC, IC=DC, K=128, wcol=wcol,
                       wq="sync",
                       rhs_fn=lambda i, e=e: x2n[i][:, TE * e:TE * (e + 1)],
                       out_ts=[(0, TE)], banks=banks, bank0=4,
                       name=f"m1_{e}", evac_s=ev_m1,
                       w_gates=wg, te_gates=[(x2s, DC)],
                       chain=(gm2_prev if e > 0 else None))

            def ev_m2(v, o, ti, ps, e=e):
                if v is None:
                    return "mine"
                eld[o].wait(v, 32 * (e + 1))
                v.scalar_tensor_tensor(out=xscr[:, 0:TE], in0=ps,
                                       scalar=c_b2[:, o:o + 1], in1=Xq[o][:, 0:TE],
                                       op0=ALU.add, op1=ALU.add)
                return v.tensor_tensor(out=Xq[o][:, 0:TE],
                                       in0=xscr[:, 0:TE], in1=xiq[o][:],
                                       op=ALU.add)

            gm2 = gemm(cx, blk, wblkb=w_m2b, OC=DC, IC=HC, K=128, wcol=wcol,
                       wq="sync", rhs_fn=lambda i: m1buf[i][:],
                       out_ts=[(0, TE)], banks=banks, bank0=6,
                       name=f"m2_{e}", evac_v=ev_m2,
                       w_gates=[(csem[3], 16), (gm1["tt"], gm1["tt_total"])],
                       rhs_gate_fn=lambda i, g=gm1: (g["ev"], i + 1),
                       chain=gm1)
            gm2_prev = gm2

            def mk_est(s, e=e, g=gm2):
                for o in range(DC):
                    g["ev"].wait(s, o + 1)
                    est[o % 2].inc(
                        s.dma_start(out=X2o[128 * o:128 * (o + 1),
                                            TE * e:TE * (e + 1)],
                                    in_=Xq[o][:, 0:TE]), 16)
            blk.sync(mk_est)
            prev = (gm2["tt"], gm2["tt_total"])

        def mk_fin(s):
            est[0].wait(s, 16 * 32)
            est[1].wait(s, 16 * 32)
        blk.sync(mk_fin)

    return cx


def host_attn0_inputs(core_inputs, full):
    coup = full["L0_attn_coupling"]
    G2 = np.zeros((128, 128), np.float32)
    for h in range(4):
        G2[16 * h:16 * h + 16, 16 * h:16 * h + 16] = coup[h]
        G2[64 + 16 * h:64 + 16 * h + 16, 64 + 16 * h:64 + 16 * h + 16] = coup[h]
    return dict(
        X1=core_inputs["X1"], XNA=core_inputs["XNA"], st1=core_inputs["st1"],
        w_in=blk_w(full["L0_attn_w_in"]),
        w_outT=fm(full["L0_attn_w_out"]),
        G2=G2,
        w_gate=blk_w(full["L0_attn_w_gate"]),
        w_mlp1=blk_w(full["L0_attn_w_mlp1"]),
        w_mlp2=blk_w(full["L0_attn_w_mlp2"]),
        ldec=full["L0_attn_log_decay"].reshape(64, 1),
        freq=full["L0_attn_frequency"].reshape(64, 1),
        b_gate=colpack(full["L0_attn_b_gate"]),
        b_m1=colpack(full["L0_attn_b_mlp1"]),
        b_m2=colpack(full["L0_attn_b_mlp2"]),
        g2c=colpack(full["L0_attn_ln2_g"]),
        b2c=colpack(full["L0_attn_ln2_b"]),
    )




TE2 = T + 128
DI = 2048
DTC = DI // 128
CAP = float(T * NCORE // 8)
CSL = [(0, 512), (512, 512), (1024, 512), (1536, 512), (2048, 128)]
USL = [(0, 512), (512, 512), (1024, 512), (1536, 512), (2048, 1)]
ZSL = [(128, 512), (640, 512), (1152, 512), (1664, 512)]


def ln_stats_stream2(cx, blk, dram, nC, nt, banks, bank0, rowstage, srow,
                     xst, xld, gates=(), tag="st2", tiles=None):
    half = nt // 2
    ts = tslices(half)
    NTS = len(ts)
    ones = tiles[0] if tiles else cx.sb(f"ones_{tag}", [128, 1], F32)
    sq = tiles[1] if tiles else cx.sb(f"sq_{tag}", [128, 512], F32)
    ssem = cx.sem("sq")
    qcons = cx.sem("qc")
    vdone = cx.sem("vd")
    done = cx.sem("sd")
    vs = cx.sem("vo")

    def mk_ld(s):
        for g, v in gates:
            g.wait(s, v)
        for p in range(2):
            c0 = p * half
            for i in range(nC):
                k = p * nC + i
                if k >= 2:
                    qcons.wait(s, (k - 1) * NTS)
                xld[k % 2].inc(
                    s.dma_start(out=xst[k % 2][:, c0:c0 + half],
                                in_=dram[128 * i:128 * (i + 1),
                                         c0:c0 + half]), 16)
            vdone.wait(s, 2 * p + 1)
            done.inc(s.dma_start(out=srow[0:1, c0:c0 + half],
                                 in_=rowstage[0:1, c0:c0 + half]), 16)
            vdone.wait(s, 2 * p + 2)
            done.inc(s.dma_start(out=srow[1:2, c0:c0 + half],
                                 in_=rowstage[0:1, c0:c0 + half]), 16)
        done.wait(s, 64)

    def mk_s(sc):
        n = 0
        for p in range(2):
            c0 = p * half
            for i in range(nC):
                k = p * nC + i
                xld[k % 2].wait(sc, 16 * (k // 2 + 1))
                for toff, tw in ts:
                    if n >= 1:
                        qcons.wait(sc, n)
                    ssem.inc(
                        sc.activation(out=sq[:, 0:tw],
                                      in_=xst[k % 2][:, c0 + toff:
                                                     c0 + toff + tw],
                                      func=AF.Square), 1)
                    n += 1

    def mk_te(te):
        vs.wait(te, 1)
        n = 0
        for p in range(2):
            c0 = p * half
            if p == 1:
                vdone.wait(te, 2)
            for i in range(nC):
                k = p * nC + i
                xld[k % 2].wait(te, 16 * (k // 2 + 1))
                for ti, (toff, tw) in enumerate(ts):
                    te.matmul(banks[bank0 + ti][0:1, 0:tw], ones[:, 0:1],
                              xst[k % 2][:, c0 + toff:c0 + toff + tw],
                              start=(i == 0), stop=(i == nC - 1))
                    ssem.wait(te, n + 1)
                    mm = te.matmul(banks[bank0 + 4 + ti][0:1, 0:tw],
                                   ones[:, 0:1], sq[:, 0:tw],
                                   start=(i == 0), stop=(i == nC - 1))
                    qcons.inc(mm, 1)
                    n += 1

    def mk_v(v):
        vs.inc(v.memset(ones[:], 1.0), 1)
        for p in range(2):
            c0 = p * half
            qcons.wait(v, (p + 1) * nC * NTS)
            for ti, (toff, tw) in enumerate(ts):
                last = v.tensor_copy(out=rowstage[0:1, c0 + toff:
                                                  c0 + toff + tw],
                                     in_=banks[bank0 + ti][0:1, 0:tw])
            vdone.inc(last, 1)
            done.wait(v, 16 * (2 * p + 1))
            for ti, (toff, tw) in enumerate(ts):
                last = v.tensor_copy(out=rowstage[0:1, c0 + toff:
                                                  c0 + toff + tw],
                                     in_=banks[bank0 + 4 + ti][0:1, 0:tw])
            vdone.inc(last, 1)

    blk.sync(mk_ld)
    blk.scalar(mk_s)
    blk.tensor(mk_te)
    blk.vector(mk_v)
    return done


def build_l2():
    cx = Cx("L2")
    X3 = cx.din("X3", [D, TE2])
    XL3 = cx.din("XL3", [D, TE2], BF16)
    rank_i = cx.din("rank", [128, 17])
    gate_i = cx.din("gate", [128, 17])
    mask_i = cx.din("mask", [128, 17 * 8])
    counts = cx.din("counts", [8, 8])
    pfx = cx.din("pfx", [8, 1])
    pfxh = cx.din("pfxh", [8, 1])
    h0 = cx.din("h0c", [128, 1])
    We = cx.din("We", [128, DC * D])
    bec = cx.din("bec", [128, DC])
    id_in = cx.din("id128", [128, 128])
    w_ip = cx.din("w_ip", [32, 128, DC, 128])
    w_xp = cx.din("w_xp", [1, 128, DTC, 128])
    cw_in = cx.din("cw", [128, DTC * 4])
    cb_in = cx.din("cb", [128, DTC])
    m_iw = cx.din("m_iw", [1, 64, 1, 128])
    m_xw = cx.din("m_xw", [1, 64, 1, 128])
    m_dw = cx.din("m_dw", [1, 4, 1, 128])
    m_cw = cx.din("m_cw", [64, 2])
    m_cb = cx.din("m_cb", [64, 1])
    m_db = cx.din("m_db", [64, 1])
    m_al = cx.din("m_al", [64, 4])

    X4o = cx.dout("X4", [D, TE2])
    XL4o = cx.dout("XL4", [D, TE2], BF16)
    Uo = cx.dout("U", [DI, T])
    Zo = cx.dout("Z", [DI, T])
    XDo = cx.dout("XD", [32, T + 1])
    FMo = cx.dout("FM", [64, 4])
    SDMo = cx.dout("SDM", [64, 1])
    DTMo = cx.dout("DTM", [64, T])
    UMo = cx.dout("UM", [64, T])
    ZMo = cx.dout("ZM", [64, T])
    XDMo = cx.dout("XDM", [12, T])

    X4i = cx.dtmp("X4i", [D, TE2], F32)
    srow = cx.dtmp("srow", [2, TE2], F32)
    prow = cx.dtmp("prow", [4, TE2], F32)
    geffD = cx.dtmp("geffD", [1, TE2], F32)
    bmD = cx.dtmp("bmD", [1, 8 * T], F32)

    es = contextlib.ExitStack()

    def sbe(tag, shape, dt):
        cx._n += 1
        return es.enter_context(
            cx.nc.sbuf_tensor(f"sb_{tag}{cx._n}", list(shape), dt))

    with cx.st:
        blk = cx.block()
        wcol = [cx.sb("wcol", [128, DC * 128], BF16) for _ in range(2)]
        wxp_sb = cx.sb("wxp", [128, DTC * 128], BF16)
        xscr = cx.sb("xscr", [128, 512], F32)
        rkc = cx.sb("rkc", [128, 17], F32)
        gtc = cx.sb("gtc", [128, 17], F32)
        mkc = cx.sb("mkc", [128, 17 * 8], F32)
        cnt_sb = cx.sb("cnt", [8, 8], F32)
        pfx_sb = cx.sb("pfxs", [8, 1], F32)
        pfxh_sb = cx.sb("pfxhs", [8, 1], F32)
        h0c = cx.sb("h0c", [128, 1], F32)
        offr = cx.sb("offr", [1, 8], F32)
        offrh = cx.sb("offrh", [1, 8], F32)
        offbc = cx.sb("offbc", [128, 8], F32)
        offbch = cx.sb("offbch", [128, 8], F32)
        gcols = cx.sb("gcols", [128, 17], F32)
        t8 = cx.sb("t8", [128, 8], F32)
        c1 = cx.sb("c1", [128, 2], F32)
        id_sb = cx.sb("id", [128, 128], F32)
        onesr = cx.sb("onesr", [1, 128], F32)
        grow_sb = cx.sb("grow", [17, 128], F32)
        becs = cx.sb("becs", [128, DC], F32)
        cws = cx.sb("cws", [128, DTC * 4], F32)
        cbs = cx.sb("cbs", [128, DTC], F32)
        mrk = cx.sb("mrk", [1, 2], F32)
        xpart = cx.sb("xpart", [128, TE2], F32)
        uf = cx.sb("uf", [128, T + 1], F32)
        ubt = cx.sb("ubt", [128, T + 1], BF16)
        xdbl = cx.sb("xdblL", [128, T + 1], F32)
        onesc = cx.sb("onesc", [128, 1], F32)
        sqst = cx.sb("sqst", [128, 512], F32)
        orow1 = cx.sb("orow1", [1, 128], F32)
        banks = [cx.ps() for _ in range(8)]
        XB = [sbe("XB", [128, TE2], BF16) for _ in range(DC)]
        geffbc = sbe("geffbc", [128, TE2], F32)
        Wesb = sbe("Wesb", [128, DC * D], BF16)
        Xq = [sbe("Xq", [128, 512], F32) for _ in range(DC)]
        rowsq = sbe("rowsq", [128, 10 * (TE2 // 128)], F32)
        rowstage = sbe("rowstage", [1, TE2], F32)
        xst = [sbe("xst", [128, TE2], F32) for _ in range(2)]
        xna = [sbe("xna", [128, 512], BF16) for _ in range(2)]
        mbc = sbe("mbc", [128, TE2], F32)
        rbc = sbe("rbc", [128, TE2], F32)
        zf = [sbe("zf", [128, T], F32) for _ in range(2)]

        ld = cx.sem("ld", pinned=True)
        xld = cx.sems(2, "xld", pinned=True)
        csem = cx.sems(6, "cast", pinned=True)
        ser_v = cx.sem("serv", pinned=True)
        ser_s = cx.sem("sers", pinned=True)

        def ch_v(v, instr):
            ser_v.inc(instr, 1)
            ser_v.wait(v, ser_v.n)
            return instr

        def ch_s(sc, instr):
            ser_s.inc(instr, 1)
            ser_s.wait(sc, ser_s.n)
            return instr

        def mark_v(v):
            return v.tensor_copy(out=mrk[0:1, 1:2], in_=mrk[0:1, 0:1])

        Web = cast_blk(cx, blk, We, csem[0])
        w_ipb = cast_blk(cx, blk, w_ip, csem[1])
        w_xpb = cast_blk(cx, blk, w_xp, csem[2])
        m_iwb = cast_blk(cx, blk, m_iw, csem[3])
        m_xwb = cast_blk(cx, blk, m_xw, csem[4])
        m_dwb = cast_blk(cx, blk, m_dw, csem[5])

        pairs = [(rkc[:, :], rank_i), (gtc[:, :], gate_i), (mkc[:, :], mask_i),
                 (cnt_sb[:, :], counts), (pfx_sb[:, :], pfx),
                 (pfxh_sb[:, :], pfxh), (h0c[:, :], h0), (id_sb[:, :], id_in),
                 (becs[:, :], bec), (cws[:, :], cw_in), (cbs[:, :], cb_in)]

        def mk_ld(s):
            for dst, srcw in pairs:
                ld.inc(s.dma_start(out=dst, in_=srcw), 16)
            for i in range(DC):
                ld.inc(s.dma_start(out=XB[i][:],
                                   in_=XL3[128 * i:128 * (i + 1), :]), 16)
        blk.sync(mk_ld)
        LD_ALL = 16 * (len(pairs) + DC)

        def mk_wesb(g):
            csem[0].wait(g, 16)
            csem[0].inc(g.dma_start(out=Wesb[:], in_=Web), 16)
            csem[2].wait(g, 16)
            csem[2].inc(g.dma_start(
                out=wxp_sb[:],
                in_=w_xpb[0].rearrange("k ic of -> k (ic of)")), 16)
        blk.gpsimd(mk_wesb)

        gp = cx.sem("gp", pinned=True)
        gv = cx.sem("gv", pinned=True)
        gst = cx.sem("gst", pinned=True)

        def mk_v0(v):
            v.memset(mrk[:], 0.0)
            gv.inc(v.memset(onesr[:], 1.0), 1)
        blk.vector(mk_v0)

        def mk_p1_te(te):
            ld.wait(te, LD_ALL)
            gv.wait(te, 1)
            gp.inc(te.matmul(banks[0][0:1, 0:8], pfx_sb[:, 0:1], cnt_sb[:, :],
                             start=True, stop=True), 1)
            gp.inc(te.matmul(banks[1][0:1, 0:8], pfxh_sb[:, 0:1],
                             cnt_sb[:, :], start=True, stop=True), 1)
            gv.wait(te, 3)
            gp.inc(te.matmul(banks[0][:, 0:8], onesr[0:1, :], offr[0:1, :],
                             start=True, stop=True), 1)
            gp.inc(te.matmul(banks[1][:, 0:8], onesr[0:1, :], offrh[0:1, :],
                             start=True, stop=True), 1)
            gv.wait(te, 20)
            gp.inc(te.transpose(banks[2][0:17, 0:128], gcols[:, 0:17],
                                id_sb[:]), 1)
        blk.tensor(mk_p1_te)

        def mk_p1_v(v):
            ld.wait(v, LD_ALL)
            gp.wait(v, 2)
            ch_v(v, v.tensor_copy(out=offr[:], in_=banks[0][0:1, 0:8]))
            ch_v(v, v.tensor_copy(out=offrh[:], in_=banks[1][0:1, 0:8]))
            gv.inc(mark_v(v), 2)
            gp.wait(v, 4)
            ch_v(v, v.tensor_copy(out=offbc[:], in_=banks[0][:, 0:8]))
            ch_v(v, v.tensor_copy(out=offbch[:], in_=banks[1][:, 0:8]))
            for g in range(17):
                ob = offbch if g == 0 else offbc
                ch_v(v, v.tensor_tensor(out=t8[:],
                                        in0=mkc[:, 8 * g:8 * (g + 1)],
                                        in1=ob[:], op=ALU.mult))
                ch_v(v, v.reduce_sum(out=c1[:, 0:1], in_=t8[:], axis=AX.X))
                ch_v(v, v.tensor_tensor(out=c1[:, 0:1], in0=c1[:, 0:1],
                                        in1=rkc[:, g:g + 1], op=ALU.add))
                ch_v(v, v.tensor_single_scalar(out=c1[:, 1:2], in_=c1[:, 0:1],
                                               scalar=CAP, op=ALU.is_le))
                ch_v(v, v.tensor_tensor(out=c1[:, 1:2], in0=c1[:, 1:2],
                                        in1=gtc[:, g:g + 1], op=ALU.mult))
                ch_v(v, v.tensor_scalar_mul(out=gcols[:, g:g + 1],
                                            in0=c1[:, 1:2], scalar1=0.5))
                gv.inc(mark_v(v), 1)
            gp.wait(v, 5)
            ch_v(v, v.tensor_copy(out=grow_sb[0:17, :],
                                  in_=banks[2][0:17, 0:128]))
            gv.inc(mark_v(v), 1)
        blk.vector(mk_p1_v)

        def mk_p1_st(s):
            gv.wait(s, 21)
            gst.inc(s.dma_start(
                out=geffD.rearrange("one (g tt) -> one g tt", g=17).squeeze(0),
                in_=grow_sb[0:17, :]), 16)
        blk.sync(mk_p1_st)

        pe_bcast(cx, blk, geffD[0:1, :], TE2, banks[3], geffbc, rowstage,
                 gates=[(gst, 16)], tag="geff", ones_t=orow1)

        barrier(cx, blk, "p1")
        cx.checkpoint()

        qst = cx.sems(2, "qst", pinned=True)
        qld = cx.sems(DC, "qld", pinned=True)
        prevg = None
        for qi, (qoff, qw) in enumerate(CSL):
            def mk_xq(s, qi=qi, qoff=qoff, qw=qw):
                for o in range(DC):
                    if qi >= 1:
                        qst[o % 2].wait(s, 16 * 8 * qi)
                    qld[o].inc(
                        s.dma_start(out=Xq[o][:, 0:qw],
                                    in_=X3[128 * o:128 * (o + 1),
                                           qoff:qoff + qw]), 16)
            blk.sync(mk_xq)

            def ev_we(v, o, ti, ps, qi=qi, qoff=qoff, qw=qw):
                if v is None:
                    return "mine"
                qld[o].wait(v, 16 * (qi + 1))
                v.scalar_tensor_tensor(out=xscr[:, 0:qw], in0=ps,
                                       scalar=becs[:, o:o + 1],
                                       in1=geffbc[:, qoff:qoff + qw],
                                       op0=ALU.add, op1=ALU.mult)
                return v.tensor_tensor(out=Xq[o][:, 0:qw], in0=Xq[o][:, 0:qw],
                                       in1=xscr[:, 0:qw], op=ALU.add)

            tg = [(csem[0], 32), (ld, LD_ALL)]
            if prevg is not None:
                tg.append((prevg["ev"], prevg["ev_total"]))
            gwe = gemm(cx, blk, OC=DC, IC=DC, K=128,
                       rhs_fn=lambda i, qoff=qoff, qw=qw:
                           XB[i][:, qoff:qoff + qw],
                       out_ts=[(0, qw)], banks=banks, bank0=0, name=f"we{qi}",
                       evac_v=ev_we,
                       lhsT_fn=lambda o, i:
                           Wesb[:, D * i + 128 * o:D * i + 128 * (o + 1)],
                       te_gates=tg, chain=prevg)
            prevg = gwe

            def mk_xst(s, qoff=qoff, qw=qw, g=gwe):
                for o in range(DC):
                    g["ev"].wait(s, o + 1)
                    qst[o % 2].inc(
                        s.dma_start(out=X4i[128 * o:128 * (o + 1),
                                            qoff:qoff + qw],
                                    in_=Xq[o][:, 0:qw]), 16)
                    qst[o % 2].inc(
                        s.dma_start(out=X4o[128 * o:128 * (o + 1),
                                            qoff:qoff + qw],
                                    in_=Xq[o][:, 0:qw]), 16)
            blk.sync(mk_xst)

        def mk_qwait(s):
            qst[0].wait(s, 16 * 8 * len(CSL))
            qst[1].wait(s, 16 * 8 * len(CSL))
        blk.sync(mk_qwait)
        barrier(cx, blk, "p2")
        cx.checkpoint()

        sdone = ln_stats_stream2(cx, blk, X4i, DC, TE2, banks, 0, rowstage,
                                 srow, xst, xld, tag="ln4",
                                 tiles=(onesc, sqst))
        rdone, rdn = ln_rows(cx, blk, srow, prow, rowsq, TE2,
                             gates=[(sdone, 64)], tag="ln4")
        d2, n2 = pe_bcast(cx, blk, prow[0:1, :], TE2, banks[0], mbc, rowstage,
                          gates=[(rdone, rdn)], tag="m4", ones_t=orow1)
        d3, n3 = pe_bcast(cx, blk, prow[1:2, :], TE2, banks[1], rbc, rowstage,
                          gates=[(rdone, rdn), (d2, n2)], tag="r4",
                          ones_t=orow1)

        xnas = cx.sems(2, "xnas")
        sts = cx.sems(2, "stx")
        xcons = cx.sem("xc")
        NLD0 = 2 * DC

        def mk_p3_ld(s):
            d2.wait(s, n2)
            d3.wait(s, n3)
            nch = 0
            for i in range(DC):
                if i >= 2:
                    xcons.wait(s, i - 1)
                xld[i % 2].inc(
                    s.dma_start(out=xst[i % 2][:],
                                in_=X4i[128 * i:128 * (i + 1), :]), 16)
                for toff, tw in tslices(TE2):
                    xnas[nch % 2].wait(s, nch // 2 + 1)
                    sts[nch % 2].inc(
                        s.dma_start(out=XL4o[128 * i:128 * (i + 1),
                                             toff:toff + tw],
                                    in_=xna[nch % 2][:, 0:tw]), 16)
                    nch += 1
            sts[0].wait(s, 16 * ((nch + 1) // 2))
            sts[1].wait(s, 16 * (nch // 2))
        blk.sync(mk_p3_ld)

        def mk_p3_v(v):
            nch = 0
            for i in range(DC):
                xld[i % 2].wait(v, 16 * ((NLD0 + i) // 2 + 1))
                ts5 = tslices(TE2)
                for ci, (toff, tw) in enumerate(ts5):
                    if nch >= 2:
                        sts[nch % 2].wait(v, 16 * ((nch - 2) // 2 + 1))
                    sub = v.tensor_tensor(out=xscr[:, 0:tw],
                                          in0=xst[i % 2][:, toff:toff + tw],
                                          in1=mbc[:, toff:toff + tw],
                                          op=ALU.subtract)
                    if ci == len(ts5) - 1:
                        xcons.inc(sub, 1)
                    xnas[nch % 2].inc(
                        v.tensor_tensor(out=xna[nch % 2][:, 0:tw],
                                        in0=xscr[:, 0:tw],
                                        in1=rbc[:, toff:toff + tw],
                                        op=ALU.mult), 1)
                    v.tensor_copy(out=XB[i][:, toff:toff + tw],
                                  in_=xna[nch % 2][:, 0:tw])
                    nch += 1
        blk.vector(mk_p3_v)

        barrier(cx, blk, "p3")
        cx.checkpoint()

        p4v = cx.sem("p4v", pinned=True)
        p4sc = cx.sem("p4sc", pinned=True)
        p4x = cx.sem("p4x", pinned=True)
        p4a = cx.sem("p4a", pinned=True)
        p4st = cx.sem("p4st", pinned=True)
        zst = cx.sem("zst", pinned=True)
        fin = cx.sem("fin", pinned=True)
        gcalls = []
        for dt_i in range(DTC):
            wg = [(csem[1], 16)]
            if dt_i >= 2:
                wg.append((gcalls[dt_i - 2]["tt"], 8))
            tg = [(p4v, dt_i * 3)] if dt_i else []

            def ev_ip(v, o, ti, ps, dt_i=dt_i):
                if v is None:
                    return "mine"
                toff, tw = CSL[ti]
                return v.tensor_copy(out=xpart[:, toff:toff + tw], in_=ps)

            gip = gemm(cx, blk, wblkb=w_ipb, o0=dt_i, OC=1, IC=DC, K=128,
                       wcol=wcol, wq="sync", wsel=dt_i % 2,
                       rhs_fn=lambda i: XB[i][:],
                       out_ts=CSL, banks=banks, bank0=0, name=f"ip{dt_i}",
                       evac_v=ev_ip, w_gates=wg, te_gates=tg,
                       chain=(gcalls[-1] if gcalls else None))
            gcalls.append(gip)

            def mk_p4_v(v, dt_i=dt_i, g=gip):
                g["ev"].wait(v, g["ev_total"])
                if dt_i >= 1:
                    p4st.wait(v, 16 * dt_i)
                base = 124
                ch_v(v, v.tensor_scalar_mul(
                    out=uf[:, 0:T + 1], in0=xpart[:, base:base + T + 1],
                    scalar1=cws[:, 4 * dt_i:4 * dt_i + 1]))
                for tau in (1, 2, 3):
                    ch_v(v, v.scalar_tensor_tensor(
                        out=uf[:, 0:T + 1],
                        in0=xpart[:, base + tau:base + tau + T + 1],
                        scalar=cws[:, 4 * dt_i + tau:4 * dt_i + tau + 1],
                        in1=uf[:, 0:T + 1], op0=ALU.mult, op1=ALU.add))
                p4v.inc(mark_v(v), 1)
                p4sc.wait(v, dt_i + 1)
                ch_v(v, v.tensor_copy(out=ubt[:], in_=uf[:, 0:T + 1]))
                p4v.inc(mark_v(v), 1)
                for ci, (coff, cw_) in enumerate(USL):
                    p4x.wait(v, dt_i * 5 + ci + 1)
                    if dt_i == 0:
                        op = v.tensor_copy(out=xdbl[:, coff:coff + cw_],
                                           in_=banks[5 + ci % 2][:, 0:cw_])
                    else:
                        op = v.tensor_tensor(out=xdbl[:, coff:coff + cw_],
                                             in0=xdbl[:, coff:coff + cw_],
                                             in1=banks[5 + ci % 2][:, 0:cw_],
                                             op=ALU.add)
                    p4a.inc(op, 1)
                p4v.inc(mark_v(v), 1)
            blk.vector(mk_p4_v)

            def mk_p4_s(sc, dt_i=dt_i):
                p4v.wait(sc, dt_i * 3 + 1)
                p4sc.inc(sc.activation(out=uf[:, 0:T + 1], in_=uf[:, 0:T + 1],
                                       func=AF.Silu,
                                       bias=cbs[:, dt_i:dt_i + 1]), 1)
            blk.scalar(mk_p4_s)

            def mk_p4_te(te, dt_i=dt_i):
                csem[2].wait(te, 32)
                p4v.wait(te, dt_i * 3 + 2)
                for ci, (coff, cw_) in enumerate(USL):
                    w = 5 * dt_i if ci < 2 else 5 * dt_i + ci - 1
                    if w > 0:
                        p4a.wait(te, w)
                    p4x.inc(te.matmul(banks[5 + ci % 2][:, 0:cw_],
                                      wxp_sb[:, 128 * dt_i:128 * (dt_i + 1)],
                                      ubt[:, coff:coff + cw_],
                                      start=True, stop=True), 1)
            blk.tensor(mk_p4_te)

            def mk_p4_st(s, dt_i=dt_i):
                p4v.wait(s, dt_i * 3 + 2)
                p4st.inc(s.dma_start(out=Uo[128 * dt_i:128 * (dt_i + 1), :],
                                     in_=uf[:, 1:T + 1]), 16)
            blk.sync(mk_p4_st)

        for zo in range(DTC):
            k = DTC + zo
            wg = [(csem[1], 16), (gcalls[k - 2]["tt"], 8)]
            tg = [(gcalls[-1]["ev"], 5 if zo == 0 else 4)]

            def ev_z(v, o, ti, ps, k=k, zo=zo):
                if v is None:
                    return "mine"
                if zo >= 2 and ti == 0:
                    zst.wait(v, 16 * (zo - 1))
                return v.tensor_copy(out=zf[k % 2][:, 512 * ti:512 * (ti + 1)],
                                     in_=ps)

            gz = gemm(cx, blk, wblkb=w_ipb, o0=DTC + zo, OC=1, IC=DC, K=128,
                      wcol=wcol, wq="sync", wsel=k % 2,
                      rhs_fn=lambda i: XB[i][:],
                      out_ts=ZSL, banks=banks, bank0=0, name=f"z{zo}",
                      evac_v=ev_z, w_gates=wg, te_gates=tg,
                      chain=gcalls[-1])
            gcalls.append(gz)

            def mk_z_st(s, zo=zo, g=gz):
                g["ev"].wait(s, g["ev_total"])
                zst.inc(s.dma_start(out=Zo[128 * zo:128 * (zo + 1), :],
                                    in_=zf[(DTC + zo) % 2][:]), 16)
            blk.sync(mk_z_st)

        def mk_p4_fin(s):
            p4v.wait(s, DTC * 3)
            p4st.wait(s, 16 * DTC)
            zst.wait(s, 16 * DTC)
            fin.inc(s.dma_start(out=XDo, in_=xdbl[64:96, :]), 16)
            fin.wait(s, 16)
        blk.sync(mk_p4_fin)
        barrier(cx, blk, "p4")
        es.close()

        dtib = cx.sb("dtib", [64, T + 1], BF16)
        xzm = cx.sb("xzm", [128, T + 1], F32)
        umt = cx.sb("umt", [64, T], F32)
        umb = cx.sb("umb", [64, T], BF16)
        xdm = cx.sb("xdm", [128, T], F32)
        xdmb = cx.sb("xdmb", [4, T], BF16)
        dtmt = cx.sb("dtmt", [64, T], F32)
        dum = cx.sb("dum", [64, T], F32)
        spa = cx.sb("spa", [64, T], F32)
        spb = cx.sb("spb", [64, T], F32)
        bbc = cx.sb("bbc", [64, T], F32)
        dam = cx.sb("dam", [64, T], F32)
        dbum = cx.sb("dbum", [64, T], F32)
        smt = cx.sb("smt", [64, T], F32)
        bro = cx.sb("bro", [1, T], F32)
        fm_sb = cx.sb("fm", [64, 4], F32)
        amc = cx.sb("amc", [64, 4], F32)
        onec64 = cx.sb("onec64", [64, 1], F32)
        z64 = cx.sb("z64", [64, 1], F32)
        mcols = cx.sb("mcols", [64, 9], F32)
        mrk_s = cx.sb("mrks", [1, 2], F32)

        p5v = cx.sem("p5v", pinned=True)
        p5s = cx.sem("p5s", pinned=True)
        p5t = cx.sem("p5t", pinned=True)
        p5d = cx.sem("p5d", pinned=True)
        p5c = cx.sem("p5c", pinned=True)
        p5b = cx.sem("p5b", pinned=True)
        p5bl = cx.sem("p5bl", pinned=True)
        fst = cx.sem("fst", pinned=True)

        def mk_p5_ld(s):
            ld.inc(s.dma_start(out=mcols[0:64, 0:4], in_=m_al), 16)
            ld.inc(s.dma_start(out=mcols[0:64, 4:6], in_=m_cw), 16)
            ld.inc(s.dma_start(out=mcols[0:64, 6:7], in_=m_cb), 16)
            ld.inc(s.dma_start(out=mcols[0:64, 7:8], in_=m_db), 16)
        blk.sync(mk_p5_ld)
        LD_P5 = LD_ALL + 64

        def mk_p5_prep(v):
            ch_v(v, v.tensor_copy(out=dtib[:], in_=xdbl[0:64, :]))
            ch_v(v, v.memset(z64[:], 0.0))
            p5v.inc(v.memset(onec64[:], 1.0), 1)
        blk.vector(mk_p5_prep)

        def mk_p5_amc(sc):
            ld.wait(sc, LD_P5)
            ch_s(sc, sc.activation(out=amc[:], in_=mcols[0:64, 0:4],
                                   func=AF.Exp))
            ch_s(sc, sc.activation(out=amc[:], in_=amc[:], func=AF.Copy,
                                   scale=-1.0))
            p5s.inc(sc.activation(out=mrk_s[0:1, 0:1], in_=amc[0:1, 0:1],
                                  func=AF.Copy), 1)
        blk.scalar(mk_p5_amc)

        def ev_xz(v, o, ti, ps):
            if v is None:
                return "mine"
            toff, tw = USL[ti]
            return v.tensor_copy(out=xzm[:, toff:toff + tw], in_=ps)

        gxz = gemm(cx, blk, wblkb=m_iwb, OC=1, IC=1, K=64, wcol=wcol,
                   wq="sync", wsel=0, rhs_fn=lambda i: dtib[:],
                   out_ts=USL, banks=banks, bank0=0, name="m_in",
                   evac_v=ev_xz,
                   w_gates=[(csem[3], 16), (gcalls[-2]["tt"], 8)],
                   te_gates=[(p5v, 1), (gcalls[-1]["ev"], 4)],
                   chain=gcalls[-1])

        def mk_p5_v2(v, g=gxz):
            g["ev"].wait(v, g["ev_total"])
            ch_v(v, v.tensor_scalar_mul(out=xzm[:, 0:1], in0=xzm[:, 0:1],
                                        scalar1=h0c[:, 0:1]))
            ch_v(v, v.tensor_scalar_mul(out=umt[:], in0=xzm[0:64, 0:T],
                                        scalar1=mcols[0:64, 4:5]))
            ch_v(v, v.scalar_tensor_tensor(out=umt[:], in0=xzm[0:64, 1:T + 1],
                                           scalar=mcols[0:64, 5:6],
                                           in1=umt[:], op0=ALU.mult,
                                           op1=ALU.add))
            p5v.inc(mark_v(v), 1)
            p5s.wait(v, 2)
            ch_v(v, v.tensor_copy(out=umb[:], in_=umt[:]))
            p5v.inc(mark_v(v), 1)
        blk.vector(mk_p5_v2)

        def mk_p5_s2(sc):
            p5v.wait(sc, 2)
            p5s.inc(sc.activation(out=umt[:], in_=umt[:], func=AF.Silu,
                                  bias=mcols[0:64, 6:7]), 1)
        blk.scalar(mk_p5_s2)

        def ev_xdm(v, o, ti, ps):
            if v is None:
                return "mine"
            toff, tw = tslices(T)[ti]
            return v.tensor_copy(out=xdm[:, toff:toff + tw], in_=ps)

        gxd = gemm(cx, blk, wblkb=m_xwb, OC=1, IC=1, K=64, wcol=wcol,
                   wq="sync", wsel=1, rhs_fn=lambda i: umb[:],
                   out_ts=tslices(T), banks=banks, bank0=0, name="m_xp",
                   evac_v=ev_xdm,
                   w_gates=[(csem[4], 16), (gcalls[-1]["tt"], 8)],
                   te_gates=[(p5v, 3)], chain=gxz)

        def mk_p5_v3(v, g=gxd):
            g["ev"].wait(v, g["ev_total"])
            ch_v(v, v.tensor_copy(out=xdmb[:], in_=xdm[0:4, :]))
            p5v.inc(mark_v(v), 1)
        blk.vector(mk_p5_v3)

        def ev_dtm(v, o, ti, ps):
            if v is None:
                return "mine"
            toff, tw = tslices(T)[ti]
            return v.tensor_copy(out=spa[:, toff:toff + tw], in_=ps[0:64, :])

        gdm = gemm(cx, blk, wblkb=m_dwb, OC=1, IC=1, K=4, wcol=wcol,
                   wq="sync", wsel=0, rhs_fn=lambda i: xdmb[:],
                   out_ts=tslices(T), banks=banks, bank0=0, name="m_dt",
                   evac_v=ev_dtm,
                   w_gates=[(csem[5], 16), (gxz["tt"], 1)],
                   te_gates=[(p5v, 4)], chain=gxd)

        def mk_p5_v4(v, g=gdm):
            g["ev"].wait(v, g["ev_total"])
            ch_v(v, v.tensor_scalar_add(out=spa[:], in0=spa[:],
                                        scalar1=mcols[0:64, 7:8]))
            ch_v(v, v.tensor_scalar_mul(out=spb[:], in0=spa[:], scalar1=-1.0))
            ch_v(v, v.tensor_tensor(out=spb[:], in0=spa[:], in1=spb[:],
                                    op=ALU.max))
            p5v.inc(mark_v(v), 1)
            p5s.wait(v, 3)
            ch_v(v, v.tensor_single_scalar(out=spa[:], in_=spa[:], scalar=0.0,
                                           op=ALU.max))
            ch_v(v, v.tensor_tensor(out=dtmt[:], in0=spa[:], in1=spb[:],
                                    op=ALU.add))
            ch_v(v, v.reduce_sum(out=mcols[0:64, 8:9], in_=dtmt[:],
                                 axis=AX.X))
            ch_v(v, v.tensor_tensor(out=dum[:], in0=dtmt[:], in1=umt[:],
                                    op=ALU.mult))
            p5v.inc(mark_v(v), 1)
        blk.vector(mk_p5_v4)

        def mk_p5_s3(sc):
            p5v.wait(sc, 5)
            ch_s(sc, sc.activation(out=spb[:], in_=spb[:], func=AF.Exp,
                                   scale=-1.0))
            ch_s(sc, sc.activation(out=spb[:], in_=spb[:], func=AF.Ln,
                                   bias=onec64[:, 0:1]))
            p5s.inc(sc.activation(out=mrk_s[0:1, 0:1], in_=spb[0:1, 0:1],
                                  func=AF.Copy), 1)
        blk.scalar(mk_p5_s3)

        def mk_p5_bc(s):
            p5v.wait(s, 4)
            p5b.inc(s.dma_start(
                out=bmD.rearrange("one (r t) -> one r t", r=8).squeeze(0),
                in_=xdm[4:12, :]), 16)
        blk.sync(mk_p5_bc)

        for n in range(4):
            def mk_bl(s, n=n):
                p5b.wait(s, 16)
                if n >= 1:
                    p5t.wait(s, n * 4)
                p5bl.inc(s.dma_start(out=bro[0:1, :],
                                     in_=bmD[0:1, T * n:T * (n + 1)]), 16)
            blk.sync(mk_bl)

            def mk_bt(te, n=n):
                p5bl.wait(te, 16 * (n + 1))
                for ci, (coff, cw_) in enumerate(tslices(T)):
                    w = 4 * n if ci < 2 else 4 * n + ci - 1
                    if w > 0:
                        p5c.wait(te, w)
                    p5t.inc(te.matmul(banks[ci % 2][0:64, 0:cw_],
                                      onesr[0:1, 0:64],
                                      bro[0:1, coff:coff + cw_],
                                      start=True, stop=True), 1)
            blk.tensor(mk_bt)

            def mk_bv(v, n=n):
                for ci, (coff, cw_) in enumerate(tslices(T)):
                    p5t.wait(v, n * 4 + ci + 1)
                    p5c.inc(v.tensor_copy(out=bbc[:, coff:coff + cw_],
                                          in_=banks[ci % 2][0:64, 0:cw_]), 1)
                p5s.wait(v, 4 + n)
                ch_v(v, v.tensor_tensor(out=dbum[:], in0=dum[:], in1=bbc[:],
                                        op=ALU.mult))
                ch_v(v, v.tensor_tensor_scan(out=smt[:], data0=dam[:],
                                             data1=dbum[:],
                                             initial=z64[:, 0:1],
                                             op0=ALU.mult, op1=ALU.add))
                ch_v(v, v.tensor_copy(out=fm_sb[:, n:n + 1],
                                      in_=smt[:, T - 1:T]))
                p5d.inc(mark_v(v), 3)
            blk.vector(mk_bv)

            def mk_bs(sc, n=n):
                p5v.wait(sc, 6)
                if n >= 1:
                    p5d.wait(sc, n * 3)
                p5s.inc(sc.activation(out=dam[:], in_=dtmt[:], func=AF.Exp,
                                      scale=amc[:, n:n + 1]), 1)
            blk.scalar(mk_bs)

        def mk_p5_st(s):
            p5d.wait(s, 12)
            p5v.wait(s, 6)
            fst.inc(s.dma_start(out=FMo, in_=fm_sb[:]), 16)
            fst.inc(s.dma_start(out=SDMo, in_=mcols[0:64, 8:9]), 16)
            fst.inc(s.dma_start(out=DTMo, in_=dtmt[:]), 16)
            fst.inc(s.dma_start(out=UMo, in_=umt[:]), 16)
            fst.inc(s.dma_start(out=ZMo, in_=xzm[64:128, 1:T + 1]), 16)
            fst.inc(s.dma_start(out=XDMo, in_=xdm[0:12, :]), 16)
            fst.wait(s, 96)
        blk.sync(mk_p5_st)

    return cx


def host_l2_inputs(core_inputs, full):
    f = full
    d = dict(core_inputs)
    d["We"] = np.ascontiguousarray(
        f["L1_ffn_We"].reshape(8, 128, 8, 128).transpose(1, 0, 2, 3)
        .reshape(128, 8 * D))
    d["bec"] = colpack(f["L1_ffn_be"])
    d["id128"] = np.eye(128, dtype=np.float32)
    d["w_ip"] = blk_w(f["L1_attn_in_proj_w"])
    d["w_xp"] = blk_w(np.ascontiguousarray(
        np.pad(f["L1_attn_x_proj_w"], ((0, 32), (0, 0)))))
    d["cw"] = np.ascontiguousarray(
        f["L1_attn_conv_w"].reshape(16, 128, 4).transpose(1, 0, 2)
        .reshape(128, 64))
    d["cb"] = colpack(f["L1_attn_conv_b"])
    d["m_iw"] = blk_w(f["L1_attn_m_in_w"], K=64)
    d["m_xw"] = blk_w(np.ascontiguousarray(
        np.pad(f["L1_attn_m_x_proj_w"], ((0, 116), (0, 0)))), K=64)
    d["m_dw"] = blk_w(np.ascontiguousarray(
        np.pad(f["L1_attn_m_dt_w"], ((0, 64), (0, 0)))), K=4)
    d["m_cw"] = np.ascontiguousarray(f["L1_attn_m_conv_w"])
    d["m_cb"] = np.ascontiguousarray(f["L1_attn_m_conv_b"][:, None])
    d["m_db"] = np.ascontiguousarray(f["L1_attn_m_dt_b"][:, None])
    d["m_al"] = np.ascontiguousarray(f["L1_attn_m_A_log"])
    return d



def build_l3():
    cx = Cx("L3")
    DTMi = cx.din("DTM", [64, T])
    UMi = cx.din("UM", [64, T])
    ZMi = cx.din("ZM", [64, T])
    XDMi = cx.din("XDM", [12, T])
    FMsi = cx.din("FMs", [64, 32])
    SDMsi = cx.din("SDMs", [64, 8])
    stepmi = cx.din("stepm", [64, 8])
    m_ali = cx.din("m_al", [64, 4])
    mDi = cx.din("mD", [64, 1])
    m_ow = cx.din("m_ow", [1, 64, 1, 128])
    dt_w = cx.din("dt_w", [16, 64, 1, 128])
    dt_b = cx.din("dt_b", [128, 16])
    alog = cx.din("alog", [128, 256])
    Ui = cx.din("U", [DI, T])
    XDi = cx.din("XD", [32, T + 1])

    YLo = cx.dout("YL", [DI, T])
    CDo = cx.dout("CD", [DI, T])
    FMo = cx.dout("FMAIN", [DI, 16])
    SDo = cx.dout("SDMAIN", [DI, 1])

    bmD = cx.dtmp("bmD", [1, 8 * T], F32)

    es = contextlib.ExitStack()

    def sbe(tag, shape, dt):
        cx._n += 1
        return es.enter_context(
            cx.nc.sbuf_tensor(f"sb_{tag}{cx._n}", list(shape), dt))

    with cx.st:
        blk = cx.block()
        wcol = [cx.sb("wcol", [128, 128], BF16) for _ in range(2)]
        dtmb = cx.sb("dtmb", [64, T], BF16)
        FMs = cx.sb("FMs", [64, 32], F32)
        SDMs = cx.sb("SDMs", [64, 8], F32)
        stepm = cx.sb("stepm", [64, 8], F32)
        amc = cx.sb("amc", [64, 4], F32)
        sent = cx.sb("sent", [64, 4], F32)
        t8b = cx.sb("t8b", [64, 8], F32)
        fmm = cx.sb("fmm", [64, 8], F32)
        scol = cx.sb("scol", [64, 1], F32)
        mDc = cx.sb("mDc", [64, 1], F32)
        z64 = cx.sb("z64", [64, 1], F32)
        onesr = cx.sb("onesr", [1, 128], F32)
        onec128 = cx.sb("onec128", [128, 1], F32)
        z128 = cx.sb("z128", [128, 1], F32)
        mrk = cx.sb("mrk", [1, 2], F32)
        mrk_s = cx.sb("mrks", [1, 2], F32)
        banks = [cx.ps() for _ in range(8)]
        dtmt = sbe("dtmt", [64, T], F32)
        umt = sbe("umt", [64, T], F32)
        zmt = sbe("zmt", [64, T], F32)
        dum = sbe("dum", [64, T], F32)
        zsil = sbe("zsil", [64, T], F32)
        ymm = sbe("ymm", [64, T], F32)
        bbc64 = sbe("bbc64", [64, T], F32)
        cbc64 = sbe("cbc64", [64, T], F32)
        dam64 = sbe("dam64", [64, T], F32)
        dbum64 = sbe("dbum64", [64, T], F32)
        smt64 = sbe("smt64", [64, T], F32)
        dtmo = sbe("dtmo", [64, T], F32)
        ymb = sbe("ymb", [64, T], BF16)
        xdm12 = sbe("xdm12", [12, T], F32)
        bbcrow = sbe("bbcrow", [1, T], F32)
        cbcrow = sbe("cbcrow", [1, T], F32)

        ld = cx.sem("ld", pinned=True)
        csm = cx.sems(2, "cast", pinned=True)
        ser_v = cx.sem("serv", pinned=True)
        ser_s = cx.sem("sers", pinned=True)

        def ch_v(v, instr):
            ser_v.inc(instr, 1)
            ser_v.wait(v, ser_v.n)
            return instr

        def ch_s(sc, instr):
            ser_s.inc(instr, 1)
            ser_s.wait(sc, ser_s.n)
            return instr

        def mark_v(v):
            return v.tensor_copy(out=mrk[0:1, 1:2], in_=mrk[0:1, 0:1])

        m_owb = cast_blk(cx, blk, m_ow, csm[0])
        dt_wb = cast_blk(cx, blk, dt_w, csm[1])

        pairs = [(dtmt[:, :], DTMi), (umt[:, :], UMi), (zmt[:, :], ZMi),
                 (xdm12[:, :], XDMi), (FMs[:, :], FMsi), (SDMs[:, :], SDMsi),
                 (stepm[:, :], stepmi), (amc[:, :], m_ali), (mDc[:, :], mDi)]

        def mk_ld(s):
            for dst, srcw in pairs:
                ld.inc(s.dma_start(out=dst, in_=srcw), 16)
        blk.sync(mk_ld)
        LD_A = 16 * len(pairs)

        av = cx.sem("av", pinned=True)
        asc = cx.sem("asc", pinned=True)
        avs = cx.sem("avs", pinned=True)
        amd = cx.sem("amd", pinned=True)
        amt = cx.sem("amt", pinned=True)
        amcp = cx.sem("amcp", pinned=True)
        abl = cx.sem("abl", pinned=True)
        ab = cx.sem("ab", pinned=True)

        def mk_v0(v):
            v.memset(mrk[:], 0.0)
            ch_v(v, v.memset(z64[:], 0.0))
            ch_v(v, v.memset(z128[:], 0.0))
            ch_v(v, v.memset(onec128[:], 1.0))
            ch_v(v, v.memset(onesr[:], 1.0))
            av.inc(mark_v(v), 1)
            ld.wait(v, LD_A)
            ch_v(v, v.tensor_tensor(out=dum[:], in0=dtmt[:], in1=umt[:],
                                    op=ALU.mult))
            av.inc(mark_v(v), 1)
        blk.vector(mk_v0)

        def mk_a0(sc):
            ld.wait(sc, LD_A)
            ch_s(sc, sc.activation(out=amc[:], in_=amc[:], func=AF.Exp))
            ch_s(sc, sc.activation(out=amc[:], in_=amc[:], func=AF.Copy,
                                   scale=-1.0))
            asc.inc(sc.activation(out=mrk_s[0:1, 0:1], in_=amc[0:1, 0:1],
                                  func=AF.Copy), 1)
            asc.inc(sc.activation(out=zsil[:], in_=zmt[:], func=AF.Silu),
                    1)
        blk.scalar(mk_a0)

        for n in range(4):
            def mk_sv1(v, n=n):
                if n == 0:
                    asc.wait(v, 1)
                ch_v(v, v.tensor_scalar_mul(out=t8b[:], in0=SDMs[:],
                                            scalar1=amc[:, n:n + 1]))
                av.inc(mark_v(v), 1)
            blk.vector(mk_sv1)

            def mk_sa1(sc, n=n):
                av.wait(sc, 3 + 2 * n)
                asc.inc(sc.activation(out=t8b[:], in_=t8b[:], func=AF.Exp),
                        1)
            blk.scalar(mk_sa1)

            def mk_sv2(v, n=n):
                asc.wait(v, 3 + n)
                ch_v(v, v.tensor_scalar_add(out=t8b[:], in0=t8b[:],
                                            scalar1=-1.0))
                ch_v(v, v.tensor_tensor(out=t8b[:], in0=t8b[:], in1=stepm[:],
                                        op=ALU.mult))
                ch_v(v, v.tensor_scalar_add(out=t8b[:], in0=t8b[:],
                                            scalar1=1.0))
                ch_v(v, v.tensor_tensor(out=fmm[:],
                                        in0=FMs[:, 8 * n:8 * (n + 1)],
                                        in1=stepm[:], op=ALU.mult))
                ch_v(v, v.tensor_copy(out=scol[:], in_=z64[:]))
                for j in range(8):
                    ch_v(v, v.tensor_tensor(out=scol[:], in0=scol[:],
                                            in1=t8b[:, j:j + 1],
                                            op=ALU.mult))
                    ch_v(v, v.tensor_tensor(out=scol[:], in0=scol[:],
                                            in1=fmm[:, j:j + 1], op=ALU.add))
                ch_v(v, v.tensor_copy(out=sent[:, n:n + 1], in_=scol[:]))
                avs.inc(mark_v(v), 1)
                av.inc(mark_v(v), 1)
            blk.vector(mk_sv2)

        def mk_ab(s):
            ld.wait(s, LD_A)
            ab.inc(s.dma_start(
                out=bmD.rearrange("one (r t) -> one r t", r=8).squeeze(0),
                in_=xdm12[4:12, :]), 16)
        blk.sync(mk_ab)

        for n in range(4):
            def mk_bl(s, n=n):
                ab.wait(s, 16)
                if n >= 1:
                    amt.wait(s, n * 8)
                abl.inc(s.dma_start(out=bbcrow[0:1, :],
                                    in_=bmD[0:1, T * n:T * (n + 1)]), 16)
                abl.inc(s.dma_start(out=cbcrow[0:1, :],
                                    in_=bmD[0:1, T * (4 + n):T * (5 + n)]),
                        16)
            blk.sync(mk_bl)

            def mk_bt(te, n=n):
                abl.wait(te, 32 * (n + 1))
                for ci, (coff, cw_) in enumerate(tslices(T)):
                    w = 8 * n + 2 * (ci - 1) if ci >= 2 else 8 * n
                    if w > 0:
                        amcp.wait(te, w)
                    amt.inc(te.matmul(banks[4 + ci % 2][0:64, 0:cw_],
                                      onesr[0:1, 0:64],
                                      bbcrow[0:1, coff:coff + cw_],
                                      start=True, stop=True), 1)
                    amt.inc(te.matmul(banks[6 + ci % 2][0:64, 0:cw_],
                                      onesr[0:1, 0:64],
                                      cbcrow[0:1, coff:coff + cw_],
                                      start=True, stop=True), 1)
            blk.tensor(mk_bt)

            def mk_bv(v, n=n):
                for ci, (coff, cw_) in enumerate(tslices(T)):
                    amt.wait(v, n * 8 + 2 * ci + 2)
                    amcp.inc(v.tensor_copy(out=bbc64[:, coff:coff + cw_],
                                           in_=banks[4 + ci % 2][0:64,
                                                                 0:cw_]), 1)
                    amcp.inc(v.tensor_copy(out=cbc64[:, coff:coff + cw_],
                                           in_=banks[6 + ci % 2][0:64,
                                                                 0:cw_]), 1)
                asc.wait(v, 7 + n)
                avs.wait(v, n + 1)
                ch_v(v, v.tensor_tensor(out=dbum64[:], in0=dum[:],
                                        in1=bbc64[:], op=ALU.mult))
                ch_v(v, v.tensor_tensor_scan(out=smt64[:], data0=dam64[:],
                                             data1=dbum64[:],
                                             initial=sent[:, n:n + 1],
                                             op0=ALU.mult, op1=ALU.add))
                ch_v(v, v.tensor_tensor(out=smt64[:], in0=smt64[:],
                                        in1=cbc64[:], op=ALU.mult))
                if n == 0:
                    ch_v(v, v.tensor_copy(out=ymm[:], in_=smt64[:]))
                else:
                    ch_v(v, v.tensor_tensor(out=ymm[:], in0=ymm[:],
                                            in1=smt64[:], op=ALU.add))
                amd.inc(mark_v(v), 3)
            blk.vector(mk_bv)

            def mk_bs(sc, n=n):
                av.wait(sc, 2)
                if n >= 1:
                    amd.wait(sc, n * 3)
                asc.inc(sc.activation(out=dam64[:], in_=dtmt[:], func=AF.Exp,
                                      scale=amc[:, n:n + 1]), 1)
            blk.scalar(mk_bs)

        def mk_ymv(v):
            amd.wait(v, 12)
            asc.wait(v, 2)
            ch_v(v, v.scalar_tensor_tensor(out=ymm[:], in0=umt[:],
                                           scalar=mDc[:, 0:1], in1=ymm[:],
                                           op0=ALU.mult, op1=ALU.add))
            ch_v(v, v.tensor_tensor(out=ymm[:], in0=ymm[:], in1=zsil[:],
                                    op=ALU.mult))
            ch_v(v, v.tensor_copy(out=ymb[:], in_=ymm[:]))
            av.inc(mark_v(v), 1)
        blk.vector(mk_ymv)
        AV_YM = 11

        def ev_mo(v, o, ti, ps):
            if v is None:
                return "mine"
            toff, tw = tslices(T)[ti]
            return v.tensor_copy(out=dtmo[:, toff:toff + tw],
                                 in_=ps[0:64, :])

        gmo = gemm(cx, blk, wblkb=m_owb, OC=1, IC=1, K=64, wcol=wcol,
                   wq="sync", wsel=0, rhs_fn=lambda i: ymb[:],
                   out_ts=tslices(T), banks=banks, bank0=0, name="m_out",
                   evac_v=ev_mo, w_gates=[(csm[0], 16)],
                   te_gates=[(av, AV_YM)])

        def mk_dtmb(v, g=gmo):
            g["ev"].wait(v, g["ev_total"])
            ch_v(v, v.tensor_copy(out=dtmb[:], in_=dtmo[:]))
            av.inc(mark_v(v), 1)
        blk.vector(mk_dtmb)

        barrier(cx, blk, "pa")
        es.close()

        spa128 = cx.sb("spa128", [128, T], F32)
        spb128 = cx.sb("spb128", [128, T], F32)
        dtt = cx.sb("dtt", [128, T], F32)
        du = cx.sb("du", [128, T], F32)
        cdt = cx.sb("cdt", [128, T], F32)
        ym128 = cx.sb("ym128", [128, T], F32)
        dam128 = [cx.sb("dam128", [128, T], F32) for _ in range(2)]
        smt128 = cx.sb("smt128", [128, T], F32)
        ones128 = cx.sb("ones128", [128, T], F32)
        ut = [cx.sb("ut", [128, T], F32) for _ in range(2)]
        bbcT = [cx.sb("bbcT", [128, T], F32) for _ in range(2)]
        cbcT = [cx.sb("cbcT", [128, T], F32) for _ in range(2)]
        fmtile = cx.sb("fmtile", [128, 16], F32)
        sdall = cx.sb("sdall", [128, 16], F32)
        dtbs = cx.sb("dtbs", [128, 16], F32)
        anegs = cx.sb("anegs", [128, 256], F32)

        bv = cx.sem("bv", pinned=True)
        bsc = cx.sem("bsc", pinned=True)
        bpm = cx.sem("bpm", pinned=True)
        vch = cx.sem("vch", pinned=True)
        bdam = cx.sem("bdam", pinned=True)
        bvn = cx.sem("bvn", pinned=True)
        rowld = cx.sem("rowld", pinned=True)
        uld = cx.sem("uld", pinned=True)
        cdst = cx.sem("cdst", pinned=True)
        sdst = cx.sem("sdst", pinned=True)
        ylst = cx.sem("ylst", pinned=True)
        fmst = cx.sem("fmst", pinned=True)

        def mk_bld(s):
            ld.inc(s.dma_start(out=dtbs[:], in_=dt_b), 16)
            ld.inc(s.dma_start(out=anegs[:], in_=alog), 16)
        blk.sync(mk_bld)
        LD_B = LD_A + 32

        def mk_b0v(v):
            ch_v(v, v.memset(ones128[:], 1.0))
            bv.inc(mark_v(v), 1)
        blk.vector(mk_b0v)

        def mk_b0a(sc):
            ld.wait(sc, LD_B)
            ch_s(sc, sc.activation(out=anegs[:], in_=anegs[:], func=AF.Exp))
            ch_s(sc, sc.activation(out=anegs[:], in_=anegs[:], func=AF.Copy,
                                   scale=-1.0))
            bsc.inc(sc.activation(out=mrk_s[0:1, 1:2], in_=anegs[0:1, 0:1],
                                  func=AF.Copy), 1)
        blk.scalar(mk_b0a)

        gB = []
        for dt_i in range(DTC):
            wg = [(csm[1], 16)]
            if dt_i >= 2:
                wg.append((gB[dt_i - 2]["tt"], 1))
            prev = gB[-1] if gB else gmo

            def ev_d(v, o, ti, ps, dt_i=dt_i):
                if v is None:
                    return "mine"
                toff, tw = tslices(T)[ti]
                return v.tensor_scalar_add(out=spa128[:, toff:toff + tw],
                                           in0=ps,
                                           scalar1=dtbs[:, dt_i:dt_i + 1])

            tg = [(prev["ev"], prev["ev_total"])]
            gd = gemm(cx, blk, wblkb=dt_wb, o0=dt_i, OC=1, IC=1, K=64,
                      wcol=wcol, wq="sync", wsel=dt_i % 2,
                      rhs_fn=lambda i: dtmb[:],
                      out_ts=tslices(T), banks=banks, bank0=0,
                      name=f"dlt{dt_i}", evac_v=ev_d, w_gates=wg,
                      te_gates=tg, chain=prev)
            gB.append(gd)

            def mk_tv1(v, dt_i=dt_i, g=gd):
                g["ev"].wait(v, g["ev_total"])
                ch_v(v, v.tensor_scalar_mul(out=spb128[:], in0=spa128[:],
                                            scalar1=-1.0))
                ch_v(v, v.tensor_tensor(out=spb128[:], in0=spa128[:],
                                        in1=spb128[:], op=ALU.max))
                bv.inc(mark_v(v), 1)
            blk.vector(mk_tv1)

            def mk_ta(sc, dt_i=dt_i):
                bv.wait(sc, 3 * dt_i + 2)
                ch_s(sc, sc.activation(out=spb128[:], in_=spb128[:],
                                       func=AF.Exp, scale=-1.0))
                ch_s(sc, sc.activation(out=spb128[:], in_=spb128[:],
                                       func=AF.Ln, bias=onec128[:, 0:1]))
                bsc.inc(sc.activation(out=mrk_s[0:1, 1:2],
                                      in_=spb128[0:1, 0:1], func=AF.Copy),
                        1)
            blk.scalar(mk_ta)

            def mk_tv2(v, dt_i=dt_i):
                bsc.wait(v, dt_i + 2)
                if dt_i >= 1:
                    bdam.wait(v, 16 * dt_i)
                ch_v(v, v.tensor_single_scalar(out=spa128[:], in_=spa128[:],
                                               scalar=0.0, op=ALU.max))
                ch_v(v, v.tensor_tensor(out=dtt[:], in0=spa128[:],
                                        in1=spb128[:], op=ALU.add))
                ch_v(v, v.reduce_sum(out=sdall[:, dt_i:dt_i + 1], in_=dtt[:],
                                     axis=AX.X))
                if dt_i >= 1:
                    cdst.wait(v, 16 * dt_i)
                ch_v(v, v.tensor_tensor_scan(out=cdt[:], data0=ones128[:],
                                             data1=dtt[:],
                                             initial=z128[:, 0:1],
                                             op0=ALU.mult, op1=ALU.add))
                bv.inc(mark_v(v), 1)
                uld.wait(v, 16 * (dt_i + 1))
                ch_v(v, v.tensor_tensor(out=du[:], in0=dtt[:],
                                        in1=ut[dt_i % 2][:], op=ALU.mult))
                bv.inc(mark_v(v), 1)
            blk.vector(mk_tv2)

            def mk_ts(s, dt_i=dt_i):
                if dt_i >= 2:
                    bv.wait(s, 3 * (dt_i - 1))
                uld.inc(s.dma_start(out=ut[dt_i % 2][:],
                                    in_=Ui[128 * dt_i:128 * (dt_i + 1), :]),
                        16)
                bv.wait(s, 3 * dt_i + 3)
                cdst.inc(s.dma_start(out=CDo[128 * dt_i:128 * (dt_i + 1), :],
                                     in_=cdt[:]), 16)
                sdst.inc(s.dma_start(out=SDo[128 * dt_i:128 * (dt_i + 1),
                                             0:1],
                                     in_=sdall[:, dt_i:dt_i + 1]), 16)
            blk.sync(mk_ts)

            def mk_rl(s, dt_i=dt_i):
                for n in range(16):
                    nb = 16 * dt_i + n
                    if nb >= 2:
                        bvn.wait(s, nb - 1)
                    rowld.inc(s.dma_start(
                        out=bbcT[n % 2][:],
                        in_=XDi[n:n + 1, 1:T + 1].partition_broadcast(128)),
                        16)
                    rowld.inc(s.dma_start(
                        out=cbcT[n % 2][:],
                        in_=XDi[16 + n:17 + n,
                                1:T + 1].partition_broadcast(128)), 16)
            blk.sync(mk_rl)

            def mk_tac(sc, dt_i=dt_i):
                bv.wait(sc, 3 * dt_i + 3)
                for n in range(16):
                    nb = 16 * dt_i + n
                    if nb >= 2:
                        bvn.wait(sc, nb - 1)
                    bdam.inc(sc.activation(
                        out=dam128[n % 2][:], in_=dtt[:], func=AF.Exp,
                        scale=anegs[:, 16 * dt_i + n:16 * dt_i + n + 1]), 1)
            blk.scalar(mk_tac)

            def mk_tvn(v, dt_i=dt_i):
                for n in range(16):
                    nb = 16 * dt_i + n
                    bdam.wait(v, nb + 1)
                    rowld.wait(v, 32 * (nb + 1))
                    if n == 0 and dt_i >= 1:
                        ylst.wait(v, 16 * dt_i)
                        fmst.wait(v, 16 * dt_i)
                    v.tensor_tensor(out=spa128[:], in0=du[:],
                                    in1=bbcT[n % 2][:], op=ALU.mult)
                    v.tensor_tensor_scan(out=smt128[:],
                                         data0=dam128[n % 2][:],
                                         data1=spa128[:],
                                         initial=z128[:, 0:1],
                                         op0=ALU.mult, op1=ALU.add)
                    ch_v(v, v.tensor_copy(out=fmtile[:, n:n + 1],
                                          in_=smt128[:, T - 1:T]))
                    v.tensor_tensor(out=spb128[:], in0=smt128[:],
                                    in1=cbcT[n % 2][:], op=ALU.mult)
                    if n == 0:
                        op = v.tensor_copy(out=ym128[:], in_=spb128[:])
                    else:
                        op = v.tensor_tensor(out=ym128[:], in0=ym128[:],
                                             in1=spb128[:], op=ALU.add)
                    bvn.inc(op, 1)
            blk.vector(mk_tvn)

            def mk_tst(s, dt_i=dt_i):
                bvn.wait(s, 16 * (dt_i + 1))
                ylst.inc(s.dma_start(out=YLo[128 * dt_i:128 * (dt_i + 1), :],
                                     in_=ym128[:]), 16)
                fmst.inc(s.dma_start(out=FMo[128 * dt_i:128 * (dt_i + 1), :],
                                     in_=fmtile[:]), 16)
            blk.sync(mk_tst)

        def mk_fin(s):
            cdst.wait(s, 16 * DTC)
            sdst.wait(s, 16 * DTC)
            ylst.wait(s, 16 * DTC)
            fmst.wait(s, 16 * DTC)
        blk.sync(mk_fin)

    return cx


def host_l3_inputs(core_inputs, full):
    f = full
    d = dict(core_inputs)
    d["m_al"] = np.ascontiguousarray(f["L1_attn_m_A_log"])
    d["mD"] = np.ascontiguousarray(f["L1_attn_m_D"][:, None])
    d["m_ow"] = blk_w(np.ascontiguousarray(
        np.pad(f["L1_attn_m_out_w"], ((0, 64), (0, 0)))), K=64)
    d["dt_w"] = blk_w(f["L1_attn_dt_out_w"], K=64)
    d["dt_b"] = colpack(f["L1_attn_dt_out_b"])
    d["alog"] = np.ascontiguousarray(
        f["L1_attn_A_log"].reshape(16, 128, 16).transpose(1, 0, 2)
        .reshape(128, 256))
    return d



def build_l4():
    cx = Cx("L4")
    YLi = cx.din("YL", [DI, T])
    CDi = cx.din("CD", [DI, T])
    Ui = cx.din("U", [DI, T])
    Zi = cx.din("Z", [DI, T])
    X4i_ = cx.din("X4", [D, T])
    XDi = cx.din("XD", [32, T + 1])
    FMAi = cx.din("FMA", [DI, 128])
    SDAi = cx.din("SDA", [DI, 8])
    stepci = cx.din("stepc", [128, 8])
    alog = cx.din("alog", [128, 256])
    dDi = cx.din("dD", [128, 16])
    op_w = cx.din("op_w", [8, 128, 16, 128])
    wg_in = cx.din("wg", [128, 8 * DC])
    tri_in = cx.din("tri", [128, 128])
    id_in = cx.din("id128", [128, 128])

    X5o = cx.dout("X5", [D, T])
    XL5o = cx.dout("XL5", [D, T], BF16)
    rk_o = cx.dout("rank", [128, 16])
    gt_o = cx.dout("gateo", [128, 16])
    mk_o = cx.dout("masko", [128, 128])
    cn_o = cx.dout("counts", [1, 8])

    X5i = cx.dtmp("X5i", [D, T], F32)
    srow = cx.dtmp("srow", [2, T], F32)
    prow = cx.dtmp("prow", [4, T], F32)

    with cx.st:
        blk = cx.block()
        ym128 = cx.sb("ym128", [128, T], F32)
        cdtT = cx.sb("cdtT", [128, T], F32)
        utt = cx.sb("utt", [128, T], F32)
        ztt = cx.sb("ztt", [128, T], F32)
        corr = [cx.sb("corr", [128, T], F32) for _ in range(2)]
        cbcT = [cx.sb("cbcT", [128, T], F32) for _ in range(2)]
        zsil = cx.sb("zsil", [128, T], F32)
        xst = [cx.sb("xst", [128, T], F32) for _ in range(2)]
        x4b = xst
        ybf = [cx.sb("ybf", [128, T], BF16) for _ in range(DTC)]
        rowstage = cx.sb("rowstage", [1, T], F32)
        mbc = cx.sb("mbc", [128, T], F32)
        rbc = cx.sb("rbc", [128, T], F32)
        xna = [cx.sb("xna", [128, 512], BF16) for _ in range(2)]
        xscr = cx.sb("xscr", [128, 512], F32)
        rowsq = cx.sb("rowsq", [128, 10 * (T // 128)], F32)
        wcol = [cx.sb("wcol", [128, DTC * 128], BF16) for _ in range(2)]
        texp = cx.sb("texp", [128, 128], F32)
        fmaT = cx.sb("fmaT", [128, 128], F32)
        sdaT = cx.sb("sdaT", [128, 8], F32)
        sent128 = cx.sb("sent", [128, 16], F32)
        t16 = cx.sb("t16", [128, 16], F32)
        stepc = cx.sb("stepc", [128, 8], F32)
        anegs = cx.sb("anegs", [128, 256], F32)
        dDs = cx.sb("dDs", [128, 16], F32)
        wg_sb = cx.sb("wg", [128, 8 * DC], F32)
        tri_sb = cx.sb("tri", [128, 128], F32)
        id_sb = cx.sb("id", [128, 128], F32)
        onesr = cx.sb("onesr", [1, 128], F32)
        mrk = cx.sb("mrk", [1, 2], F32)
        mrk_s = cx.sb("mrks", [1, 2], F32)
        banks = [cx.ps() for _ in range(8)]

        ld = cx.sem("ld", pinned=True)
        xld = cx.sems(2, "xld", pinned=True)
        csm = cx.sem("cast", pinned=True)
        ser_v = cx.sem("serv", pinned=True)
        ser_s = cx.sem("sers", pinned=True)

        def ch_v(v, instr):
            ser_v.inc(instr, 1)
            ser_v.wait(v, ser_v.n)
            return instr

        def ch_s(sc, instr):
            ser_s.inc(instr, 1)
            ser_s.wait(sc, ser_s.n)
            return instr

        def mark_v(v):
            return v.tensor_copy(out=mrk[0:1, 1:2], in_=mrk[0:1, 0:1])

        op_wb = cast_blk(cx, blk, op_w, csm)

        pairs = [(anegs[:, :], alog), (dDs[:, :], dDi), (stepc[:, :], stepci),
                 (wg_sb[:, :], wg_in), (tri_sb[:, :], tri_in),
                 (id_sb[:, :], id_in)]

        def mk_ld(s):
            for dst, srcw in pairs:
                ld.inc(s.dma_start(out=dst, in_=srcw), 16)
        blk.sync(mk_ld)
        LD0 = 16 * len(pairs)

        asc2 = cx.sem("asc2", pinned=True)
        av2 = cx.sem("av2", pinned=True)
        ventry = cx.sem("ventry", pinned=True)
        bcor = cx.sem("bcor", pinned=True)
        vch4 = cx.sem("vch4", pinned=True)
        pmm2 = cx.sem("pmm2", pinned=True)
        rowld2 = cx.sem("rowld2", pinned=True)
        bvn2 = cx.sem("bvn2", pinned=True)
        azs = cx.sem("azs", pinned=True)
        ytile = cx.sem("ytile", pinned=True)
        tld = cx.sem("tld", pinned=True)
        x4ld = cx.sems(2, "x4ld", pinned=True)
        x5st = cx.sem("x5st", pinned=True)

        def mk_v0(v):
            v.memset(mrk[:], 0.0)
            av2.inc(v.memset(onesr[:], 1.0), 1)
        blk.vector(mk_v0)

        def mk_a0(sc):
            ld.wait(sc, LD0)
            ch_s(sc, sc.activation(out=anegs[:], in_=anegs[:], func=AF.Exp))
            ch_s(sc, sc.activation(out=anegs[:], in_=anegs[:], func=AF.Copy,
                                   scale=-1.0))
            asc2.inc(sc.activation(out=mrk_s[0:1, 0:1], in_=anegs[0:1, 0:1],
                                   func=AF.Copy), 1)
        blk.scalar(mk_a0)

        for dt_i in range(DTC):
            def mk_tld(s, dt_i=dt_i):
                if dt_i >= 1:
                    ytile.wait(s, dt_i)
                sl = slice(128 * dt_i, 128 * (dt_i + 1))
                for dst, srcw in [(ym128, YLi), (cdtT, CDi), (utt, Ui),
                                  (ztt, Zi), (fmaT, FMAi), (sdaT, SDAi)]:
                    tld.inc(s.dma_start(out=dst[:], in_=srcw[sl, :]), 16)
            blk.sync(mk_tld)

            def mk_ev(v, dt_i=dt_i):
                tld.wait(v, 96 * (dt_i + 1))
                if dt_i == 0:
                    asc2.wait(v, 1)
                ch_v(v, v.memset(sent128[:], 0.0))
                for j in range(8):
                    ch_v(v, v.tensor_scalar_mul(
                        out=texp[:, 16 * j:16 * (j + 1)],
                        in0=anegs[:, 16 * dt_i:16 * (dt_i + 1)],
                        scalar1=sdaT[:, j:j + 1]))
                av2.inc(mark_v(v), 1)
                asc2.wait(v, dt_i + 2)
                for j in range(8):
                    blkj = texp[:, 16 * j:16 * (j + 1)]
                    ch_v(v, v.tensor_scalar_add(out=blkj, in0=blkj,
                                                scalar1=-1.0))
                    ch_v(v, v.tensor_scalar_mul(out=blkj, in0=blkj,
                                                scalar1=stepc[:, j:j + 1]))
                    ch_v(v, v.tensor_scalar_add(out=blkj, in0=blkj,
                                                scalar1=1.0))
                    ch_v(v, v.tensor_tensor(out=sent128[:], in0=sent128[:],
                                            in1=blkj, op=ALU.mult))
                    ch_v(v, v.tensor_scalar_mul(
                        out=t16[:], in0=fmaT[:, 16 * j:16 * (j + 1)],
                        scalar1=stepc[:, j:j + 1]))
                    ch_v(v, v.tensor_tensor(out=sent128[:], in0=sent128[:],
                                            in1=t16[:], op=ALU.add))
                ventry.inc(mark_v(v), 1)
                av2.inc(mark_v(v), 1)
            blk.vector(mk_ev)

            def mk_ea(sc, dt_i=dt_i):
                av2.wait(sc, 2 + 2 * dt_i)
                asc2.inc(sc.activation(out=texp[:], in_=texp[:],
                                       func=AF.Exp), 1)
                azs.inc(sc.activation(out=zsil[:], in_=ztt[:],
                                      func=AF.Silu), 1)
            blk.scalar(mk_ea)

            def mk_rl(s, dt_i=dt_i):
                for n in range(16):
                    nb = 16 * dt_i + n
                    if nb >= 2:
                        bvn2.wait(s, nb - 1)
                    rowld2.inc(s.dma_start(
                        out=cbcT[n % 2][:],
                        in_=XDi[16 + n:17 + n,
                                1:T + 1].partition_broadcast(128)), 16)
            blk.sync(mk_rl)

            def mk_tac(sc, dt_i=dt_i):
                tld.wait(sc, 96 * (dt_i + 1))
                for n in range(16):
                    nb = 16 * dt_i + n
                    if nb >= 2:
                        bvn2.wait(sc, nb - 1)
                    bcor.inc(sc.activation(
                        out=corr[n % 2][:], in_=cdtT[:], func=AF.Exp,
                        scale=anegs[:, 16 * dt_i + n:16 * dt_i + n + 1]), 1)
            blk.scalar(mk_tac)

            def mk_tvn(v, dt_i=dt_i):
                ventry.wait(v, dt_i + 1)
                for n in range(16):
                    nb = 16 * dt_i + n
                    bcor.wait(v, nb + 1)
                    rowld2.wait(v, 16 * (nb + 1))
                    v.scalar_tensor_tensor(
                        out=corr[n % 2][:], in0=corr[n % 2][:],
                        scalar=sent128[:, n:n + 1], in1=cbcT[n % 2][:],
                        op0=ALU.mult, op1=ALU.mult)
                    bvn2.inc(v.tensor_tensor(out=ym128[:], in0=ym128[:],
                                             in1=corr[n % 2][:],
                                             op=ALU.add), 1)
                azs.wait(v, dt_i + 1)
                ch_v(v, v.scalar_tensor_tensor(out=ym128[:], in0=utt[:],
                                               scalar=dDs[:, dt_i:dt_i + 1],
                                               in1=ym128[:], op0=ALU.mult,
                                               op1=ALU.add))
                ch_v(v, v.tensor_tensor(out=ym128[:], in0=ym128[:],
                                        in1=zsil[:], op=ALU.mult))
                ch_v(v, v.tensor_copy(out=ybf[dt_i][:], in_=ym128[:]))
                ytile.inc(mark_v(v), 1)
            blk.vector(mk_tvn)

        wl4 = cx.sems(2, "wl4")
        tt4 = cx.sem("tt4")
        ev4 = cx.sem("ev4")

        def wld(s, o):
            wl4[o % 2].inc(
                s.dma_start(out=wcol[o % 2][:, :],
                            in_=op_wb[o].rearrange("k ic of -> k (ic of)")),
                16)

        def x4load(s, o):
            x4ld[o % 2].inc(
                s.dma_start(out=x4b[o % 2][:],
                            in_=X4i_[128 * o:128 * (o + 1), :]), 16)

        def mk_wx(s):
            csm.wait(s, 16)
            wld(s, 0)
            wld(s, 1)
            x4load(s, 0)
            x4load(s, 1)
            for o in range(DC):
                if o + 2 < DC:
                    tt4.wait(s, 16 * (o + 1))
                    wld(s, o + 2)
                ev4.wait(s, 4 * o + 4)
                x5st.inc(s.dma_start(out=X5o[128 * o:128 * (o + 1), :],
                                     in_=x4b[o % 2][:]), 16)
                x5st.inc(s.dma_start(out=X5i[128 * o:128 * (o + 1), :],
                                     in_=x4b[o % 2][:]), 16)
                if o + 2 < DC:
                    x5st.wait(s, 32 * (o + 1))
                    x4load(s, o + 2)
            x5st.wait(s, 32 * DC)
        blk.sync(mk_wx)

        def mk_te4(te):
            ytile.wait(te, DTC)
            for o in range(DC):
                wl4[o % 2].wait(te, 16 * (o // 2 + 1))
                if o >= 2:
                    ev4.wait(te, (o - 1) * 4)
                for i in range(DTC):
                    for ti, (toff, tw) in enumerate(tslices(T)):
                        mm = te.matmul(banks[(o % 2) * 4 + ti][:, 0:tw],
                                       wcol[o % 2][:, 128 * i:128 * (i + 1)],
                                       ybf[i][:, toff:toff + tw],
                                       start=(i == 0), stop=(i == DTC - 1))
                tt4.inc(mm, DTC)
        blk.tensor(mk_te4)

        def mk_ev4(v):
            for o in range(DC):
                tt4.wait(v, 16 * (o + 1))
                x4ld[o % 2].wait(v, 16 * (o // 2 + 1))
                for ti, (toff, tw) in enumerate(tslices(T)):
                    ev4.inc(v.tensor_tensor(
                        out=x4b[o % 2][:, toff:toff + tw],
                        in0=banks[(o % 2) * 4 + ti][:, 0:tw],
                        in1=x4b[o % 2][:, toff:toff + tw], op=ALU.add), 1)
        blk.vector(mk_ev4)

        barrier(cx, blk, "pc")
        cx.checkpoint()

        sdone = ln_stats_stream(cx, blk, X5i, DC, T, banks, 0, rowstage,
                                srow, xst, xld, tag="ln5")
        rdone, rdn = ln_rows(cx, blk, srow, prow, rowsq, T,
                             gates=[(sdone, 32)], tag="ln5")
        d2, n2 = pe_bcast(cx, blk, prow[0:1, :], T, banks[0], mbc, rowstage,
                          gates=[(rdone, rdn)], tag="m5", ones_t=onesr)
        d3, n3 = pe_bcast(cx, blk, prow[1:2, :], T, banks[1], rbc, rowstage,
                          gates=[(rdone, rdn), (d2, n2)], tag="r5",
                          ones_t=onesr)

        xnas = cx.sems(2, "xnas")
        sts = cx.sems(2, "stx")
        xcons = cx.sem("xc")

        def mk_p3_ld(s):
            d2.wait(s, n2)
            d3.wait(s, n3)
            nch = 0
            for i in range(DC):
                if i >= 2:
                    xcons.wait(s, i - 1)
                xld[i % 2].inc(
                    s.dma_start(out=xst[i % 2][:],
                                in_=X5i[128 * i:128 * (i + 1), :]), 16)
                for toff, tw in tslices(T):
                    xnas[nch % 2].wait(s, nch // 2 + 1)
                    sts[nch % 2].inc(
                        s.dma_start(out=XL5o[128 * i:128 * (i + 1),
                                             toff:toff + tw],
                                    in_=xna[nch % 2][:, 0:tw]), 16)
                    nch += 1
            sts[0].wait(s, 16 * ((nch + 1) // 2))
            sts[1].wait(s, 16 * (nch // 2))
        blk.sync(mk_p3_ld)

        def mk_p3_v(v):
            nch = 0
            for i in range(DC):
                xld[i % 2].wait(v, 16 * ((DC + i) // 2 + 1))
                ts4 = tslices(T)
                for ci, (toff, tw) in enumerate(ts4):
                    if nch >= 2:
                        sts[nch % 2].wait(v, 16 * ((nch - 2) // 2 + 1))
                    sub = v.tensor_tensor(out=xscr[:, 0:tw],
                                          in0=xst[i % 2][:, toff:toff + tw],
                                          in1=mbc[:, toff:toff + tw],
                                          op=ALU.subtract)
                    if ci == len(ts4) - 1:
                        xcons.inc(sub, 1)
                    xnas[nch % 2].inc(
                        v.tensor_tensor(out=xna[nch % 2][:, 0:tw],
                                        in0=xscr[:, 0:tw],
                                        in1=rbc[:, toff:toff + tw],
                                        op=ALU.mult), 1)
                    nch += 1
        blk.vector(mk_p3_v)

        barrier(cx, blk, "pR")
        cx.checkpoint()
        router_block(cx, blk, X5i, prow, wg_sb, tri_sb, id_sb,
                     rowstage, banks, rk_o, gt_o, mk_o, cn_o)

    return cx


def host_l4_inputs(core_inputs, full):
    f = full
    d = dict(core_inputs)
    d["alog"] = np.ascontiguousarray(
        f["L1_attn_A_log"].reshape(16, 128, 16).transpose(1, 0, 2)
        .reshape(128, 256))
    d["dD"] = colpack(f["L1_attn_D"])
    d["op_w"] = blk_w(f["L1_attn_out_proj_w"])
    wg = f["L1_ffn_Wg"]
    d["wg"] = np.ascontiguousarray(
        wg.reshape(DC, 128, 8).transpose(1, 0, 2).reshape(128, 8 * DC))
    d["tri"] = np.triu(np.ones((128, 128), np.float32))
    d["id128"] = np.eye(128, dtype=np.float32)
    return d



def build_l5():
    cx = Cx("L5")
    X5 = cx.din("X5", [D, T])
    XL5 = cx.din("XL5", [D, T], BF16)
    rank_i = cx.din("rank", [128, 16])
    gate_i = cx.din("gate", [128, 16])
    mask_i = cx.din("mask", [128, 16 * 8])
    counts = cx.din("counts", [8, 8])
    pfx = cx.din("pfx", [8, 1])
    We = cx.din("We", [128, DC * D])
    bec = cx.din("bec", [128, DC])
    id_in = cx.din("id128", [128, 128])

    X6o = cx.dout("X6", [D, T])
    geffD = cx.dtmp("geffD", [1, T], F32)

    with cx.st:
        blk = cx.block()
        XB = [cx.sb("XB", [128, T], BF16) for _ in range(DC)]
        geffbc = cx.sb("geffbc", [128, T], F32)
        Wesb = cx.sb("Wesb", [128, DC * D], BF16)
        Xq = [cx.sb("Xq", [128, 512], F32) for _ in range(DC)]
        xscr = cx.sb("xscr", [128, 512], F32)
        rowstage = cx.sb("rowstage", [1, T], F32)
        rkc = cx.sb("rkc", [128, 16], F32)
        gtc = cx.sb("gtc", [128, 16], F32)
        mkc = cx.sb("mkc", [128, 16 * 8], F32)
        cnt_sb = cx.sb("cnt", [8, 8], F32)
        pfx_sb = cx.sb("pfxs", [8, 1], F32)
        offr = cx.sb("offr", [1, 8], F32)
        offbc = cx.sb("offbc", [128, 8], F32)
        gcols = cx.sb("gcols", [128, 16], F32)
        t8 = cx.sb("t8", [128, 8], F32)
        c1 = cx.sb("c1", [128, 2], F32)
        id_sb = cx.sb("id", [128, 128], F32)
        onesr = cx.sb("onesr", [1, 128], F32)
        grow_sb = cx.sb("grow", [16, 128], F32)
        becs = cx.sb("becs", [128, DC], F32)
        mrk = cx.sb("mrk", [1, 2], F32)
        banks = [cx.ps() for _ in range(8)]

        ld = cx.sem("ld", pinned=True)
        csem = cx.sem("cast", pinned=True)
        ser_v = cx.sem("serv", pinned=True)

        def ch_v(v, instr):
            ser_v.inc(instr, 1)
            ser_v.wait(v, ser_v.n)
            return instr

        def mark_v(v):
            return v.tensor_copy(out=mrk[0:1, 1:2], in_=mrk[0:1, 0:1])

        Web = cast_blk(cx, blk, We, csem)

        pairs = [(rkc[:, :], rank_i), (gtc[:, :], gate_i), (mkc[:, :], mask_i),
                 (cnt_sb[:, :], counts), (pfx_sb[:, :], pfx),
                 (id_sb[:, :], id_in), (becs[:, :], bec)]

        def mk_ld(s):
            for dst, srcw in pairs:
                ld.inc(s.dma_start(out=dst, in_=srcw), 16)
            for i in range(DC):
                ld.inc(s.dma_start(out=XB[i][:],
                                   in_=XL5[128 * i:128 * (i + 1), :]), 16)
        blk.sync(mk_ld)
        LD_ALL = 16 * (len(pairs) + DC)

        def mk_wesb(g):
            csem.wait(g, 16)
            csem.inc(g.dma_start(out=Wesb[:], in_=Web), 16)
        blk.gpsimd(mk_wesb)

        gp = cx.sem("gp", pinned=True)
        gv = cx.sem("gv", pinned=True)
        gst = cx.sem("gst", pinned=True)

        def mk_v0(v):
            v.memset(mrk[:], 0.0)
            gv.inc(v.memset(onesr[:], 1.0), 1)
        blk.vector(mk_v0)

        def mk_p1_te(te):
            ld.wait(te, LD_ALL)
            gv.wait(te, 1)
            gp.inc(te.matmul(banks[0][0:1, 0:8], pfx_sb[:, 0:1], cnt_sb[:, :],
                             start=True, stop=True), 1)
            gv.wait(te, 2)
            gp.inc(te.matmul(banks[0][:, 0:8], onesr[0:1, :], offr[0:1, :],
                             start=True, stop=True), 1)
            gv.wait(te, 18)
            gp.inc(te.transpose(banks[2][0:16, 0:128], gcols[:, 0:16],
                                id_sb[:]), 1)
        blk.tensor(mk_p1_te)

        def mk_p1_v(v):
            ld.wait(v, LD_ALL)
            gp.wait(v, 1)
            ch_v(v, v.tensor_copy(out=offr[:], in_=banks[0][0:1, 0:8]))
            gv.inc(mark_v(v), 1)
            gp.wait(v, 2)
            ch_v(v, v.tensor_copy(out=offbc[:], in_=banks[0][:, 0:8]))
            for g in range(16):
                ch_v(v, v.tensor_tensor(out=t8[:],
                                        in0=mkc[:, 8 * g:8 * (g + 1)],
                                        in1=offbc[:], op=ALU.mult))
                ch_v(v, v.reduce_sum(out=c1[:, 0:1], in_=t8[:], axis=AX.X))
                ch_v(v, v.tensor_tensor(out=c1[:, 0:1], in0=c1[:, 0:1],
                                        in1=rkc[:, g:g + 1], op=ALU.add))
                ch_v(v, v.tensor_single_scalar(out=c1[:, 1:2], in_=c1[:, 0:1],
                                               scalar=CAP, op=ALU.is_le))
                ch_v(v, v.tensor_tensor(out=c1[:, 1:2], in0=c1[:, 1:2],
                                        in1=gtc[:, g:g + 1], op=ALU.mult))
                ch_v(v, v.tensor_scalar_mul(out=gcols[:, g:g + 1],
                                            in0=c1[:, 1:2], scalar1=0.5))
                gv.inc(mark_v(v), 1)
            gp.wait(v, 3)
            ch_v(v, v.tensor_copy(out=grow_sb[0:16, :],
                                  in_=banks[2][0:16, 0:128]))
            gv.inc(mark_v(v), 1)
        blk.vector(mk_p1_v)

        def mk_p1_st(s):
            gv.wait(s, 19)
            gst.inc(s.dma_start(
                out=geffD.rearrange("one (g tt) -> one g tt", g=16).squeeze(0),
                in_=grow_sb[0:16, :]), 16)
        blk.sync(mk_p1_st)

        pe_bcast(cx, blk, geffD[0:1, :], T, banks[3], geffbc, rowstage,
                 gates=[(gst, 16)], tag="geff", ones_t=onesr)

        barrier(cx, blk, "p1")
        cx.checkpoint()

        qst = cx.sems(2, "qst", pinned=True)
        qld = cx.sems(DC, "qld", pinned=True)
        prevg = None
        for qi, (qoff, qw) in enumerate(tslices(T)):
            def mk_xq(s, qi=qi, qoff=qoff, qw=qw):
                for o in range(DC):
                    if qi >= 1:
                        qst[o % 2].wait(s, 16 * 4 * qi)
                    qld[o].inc(
                        s.dma_start(out=Xq[o][:, 0:qw],
                                    in_=X5[128 * o:128 * (o + 1),
                                           qoff:qoff + qw]), 16)
            blk.sync(mk_xq)

            def ev_we(v, o, ti, ps, qi=qi, qoff=qoff, qw=qw):
                if v is None:
                    return "mine"
                qld[o].wait(v, 16 * (qi + 1))
                v.scalar_tensor_tensor(out=xscr[:, 0:qw], in0=ps,
                                       scalar=becs[:, o:o + 1],
                                       in1=geffbc[:, qoff:qoff + qw],
                                       op0=ALU.add, op1=ALU.mult)
                return v.tensor_tensor(out=Xq[o][:, 0:qw], in0=Xq[o][:, 0:qw],
                                       in1=xscr[:, 0:qw], op=ALU.add)

            tg = [(csem, 32), (ld, LD_ALL)]
            if prevg is not None:
                tg.append((prevg["ev"], prevg["ev_total"]))
            gwe = gemm(cx, blk, OC=DC, IC=DC, K=128,
                       rhs_fn=lambda i, qoff=qoff, qw=qw:
                           XB[i][:, qoff:qoff + qw],
                       out_ts=[(0, qw)], banks=banks, bank0=0, name=f"we{qi}",
                       evac_v=ev_we,
                       lhsT_fn=lambda o, i:
                           Wesb[:, D * i + 128 * o:D * i + 128 * (o + 1)],
                       te_gates=tg, chain=prevg)
            prevg = gwe

            def mk_xst(s, qoff=qoff, qw=qw, g=gwe):
                for o in range(DC):
                    g["ev"].wait(s, o + 1)
                    qst[o % 2].inc(
                        s.dma_start(out=X6o[128 * o:128 * (o + 1),
                                            qoff:qoff + qw],
                                    in_=Xq[o][:, 0:qw]), 16)
            blk.sync(mk_xst)

        def mk_qwait(s):
            qst[0].wait(s, 16 * 4 * 4)
            qst[1].wait(s, 16 * 4 * 4)
        blk.sync(mk_qwait)

    return cx


def host_l5_inputs(core_inputs, full):
    f = full
    d = dict(core_inputs)
    d["We"] = np.ascontiguousarray(
        f["L1_ffn_We"].reshape(8, 128, 8, 128).transpose(1, 0, 2, 3)
        .reshape(128, 8 * D))
    d["bec"] = colpack(f["L1_ffn_be"])
    d["id128"] = np.eye(128, dtype=np.float32)
    return d


def host_ffn0_inputs(core_inputs, full, inner_prep, router):
    w1 = full["L0_ffn_w1_w"]; w2 = full["L0_ffn_w2_w"]
    w12 = np.concatenate([w1.reshape(HC, 128, D), w2.reshape(HC, 128, D)],
                         axis=1).reshape(2 * HID, D)
    d = dict(
        X0=core_inputs["X0"],
        w12=blk_w(w12), h1=blk_w(full["L0_ffn_h1_w"]),
        h2=blk_w(full["L0_ffn_h2_w"]), w3=blk_w(full["L0_ffn_w3_w"]),
        s1=colpack(full["L0_ffn_w1_s"]), s2=colpack(full["L0_ffn_w2_s"]),
        sh1=colpack(full["L0_ffn_h1_s"]), sh2=colpack(full["L0_ffn_h2_s"]),
        s3=colpack(full["L0_ffn_w3_s"]),
    )
    if inner_prep:
        d["g1"] = colpack(full["L0_attn_ln1_g"])
        d["b1"] = colpack(full["L0_attn_ln1_b"])
    if router:
        wg = full["L1_ffn_Wg"]
        d["wg"] = np.ascontiguousarray(
            wg.reshape(DC, 128, 8).transpose(1, 0, 2).reshape(128, 8 * DC))
        d["tri"] = np.triu(np.ones((128, 128), np.float32))
        d["id128"] = np.eye(128, dtype=np.float32)
    return d


def host_L1a_inputs(core_inputs, full):
    return host_ffn0_inputs(core_inputs, full, True, False)


def kernel(**inputs):
    full = inputs
    NC = NCORE
    x = np.ascontiguousarray(np.asarray(full["L0_attn_x"][0], np.float32))

    def launch(cx, maps, outs):
        r = run_bass_kernel_spmd(cx.nc, maps, core_ids=list(range(NC)))
        return [{k: np.asarray(r.results[j][k]) for k in outs}
                for j in range(NC)]

    cxa = build_L1a()
    maps = [host_L1a_inputs(dict(X0=fm(x[T * j:T * (j + 1)])), full)
            for j in range(NC)]
    ra = launch(cxa, maps, ["X1", "XNA", "st1"])

    import ml_dtypes
    cxb = build_attn0()
    maps = []
    for j in range(NC):
        halo = (np.zeros((D, 1), ml_dtypes.bfloat16) if j == 0
                else ra[j - 1]["XNA"][:, T - 1:T])
        xna_h = np.ascontiguousarray(
            np.concatenate([halo, ra[j]["XNA"]], axis=1))
        maps.append(host_attn0_inputs(
            dict(X1=ra[j]["X1"], XNA=xna_h, st1=ra[j]["st1"]), full))
    rb = launch(cxb, maps, ["X2"])

    cxc = build_ffn0(inner_prep=False, router=True)
    maps = [host_ffn0_inputs(dict(X0=rb[j]["X2"]), full, False, True)
            for j in range(NC)]
    rc = launch(cxc, maps, ["X1", "XL3", "rank", "gateo", "masko", "counts"])

    counts1 = np.ascontiguousarray(
        np.concatenate([rc[j]["counts"] for j in range(NC)], axis=0))

    cx2 = build_l2()
    maps = []
    for j in range(NC):
        X3e = np.zeros((D, TE2), np.float32)
        X3e[:, 128:] = rc[j]["X1"]
        XL3e = np.zeros((D, TE2), ml_dtypes.bfloat16)
        XL3e[:, 128:] = rc[j]["XL3"]
        rank_i = np.zeros((128, 17), np.float32)
        rank_i[:, 1:] = rc[j]["rank"]
        gate_i = np.zeros((128, 17), np.float32)
        gate_i[:, 1:] = rc[j]["gateo"]
        mask_i = np.zeros((128, 136), np.float32)
        mask_i[:, 8:] = rc[j]["masko"]
        if j >= 1:
            X3e[:, 124:128] = rc[j - 1]["X1"][:, T - 4:]
            XL3e[:, 124:128] = rc[j - 1]["XL3"][:, T - 4:]
            rank_i[:, 0] = rc[j - 1]["rank"][:, 15]
            gate_i[:, 0] = rc[j - 1]["gateo"][:, 15]
            mask_i[:, 0:8] = rc[j - 1]["masko"][:, 120:128]
        core = dict(
            X3=X3e, XL3=XL3e, rank=rank_i, gate=gate_i, mask=mask_i,
            counts=counts1,
            pfx=(np.arange(8)[:, None] < j).astype(np.float32),
            pfxh=(np.arange(8)[:, None] < j - 1).astype(np.float32),
            h0c=np.full((128, 1), 0.0 if j == 0 else 1.0, np.float32))
        maps.append(host_l2_inputs(core, full))
    r2 = launch(cx2, maps, ["X4", "XL4", "U", "Z", "XD", "FM", "SDM",
                            "DTM", "UM", "ZM", "XDM"])

    cx3 = build_l3()
    FMs_all = np.zeros((64, 32), np.float32)
    SDMs_all = np.zeros((64, 8), np.float32)
    for j in range(NC):
        for n in range(4):
            FMs_all[:, 8 * n + j] = r2[j]["FM"][:, n]
        SDMs_all[:, j] = r2[j]["SDM"][:, 0]
    maps = []
    for j in range(NC):
        core = dict(DTM=r2[j]["DTM"], UM=r2[j]["UM"], ZM=r2[j]["ZM"],
                    XDM=r2[j]["XDM"], U=r2[j]["U"], XD=r2[j]["XD"],
                    FMs=FMs_all, SDMs=SDMs_all,
                    stepm=np.ascontiguousarray(np.broadcast_to(
                        (np.arange(8) < j).astype(np.float32), (64, 8))))
        maps.append(host_l3_inputs(core, full))
    r3 = launch(cx3, maps, ["YL", "CD", "FMAIN", "SDMAIN"])

    cx4 = build_l4()
    FMA_all = np.zeros((DI, 128), np.float32)
    SDA_all = np.zeros((DI, 8), np.float32)
    for j in range(NC):
        FMA_all[:, 16 * j:16 * (j + 1)] = r3[j]["FMAIN"]
        SDA_all[:, j] = r3[j]["SDMAIN"][:, 0]
    maps = []
    for j in range(NC):
        core = dict(YL=r3[j]["YL"], CD=r3[j]["CD"], U=r2[j]["U"],
                    Z=r2[j]["Z"], X4=np.ascontiguousarray(
                        r2[j]["X4"][:, 128:]),
                    XD=r2[j]["XD"], FMA=FMA_all, SDA=SDA_all,
                    stepc=np.ascontiguousarray(np.broadcast_to(
                        (np.arange(8) < j).astype(np.float32), (128, 8))))
        maps.append(host_l4_inputs(core, full))
    r4 = launch(cx4, maps, ["X5", "XL5", "rank", "gateo", "masko", "counts"])

    counts2 = np.ascontiguousarray(
        np.concatenate([r4[j]["counts"] for j in range(NC)], axis=0))

    cx5 = build_l5()
    maps = []
    for j in range(NC):
        core = dict(X5=r4[j]["X5"], XL5=r4[j]["XL5"], rank=r4[j]["rank"],
                    gate=r4[j]["gateo"], mask=r4[j]["masko"],
                    counts=counts2,
                    pfx=(np.arange(8)[:, None] < j).astype(np.float32))
        maps.append(host_l5_inputs(core, full))
    r5 = launch(cx5, maps, ["X6"])

    out = np.concatenate([r5[j]["X6"].T for j in range(NC)], axis=0)
    return np.ascontiguousarray(out[None]).astype(np.float32)
